# version 08
# baseline (speedup 1.0000x reference)
import contextlib
import numpy as np
import ml_dtypes
import concourse.bass as bass
import concourse.mybir as mybir
from concourse.bass_utils import run_bass_kernel_spmd

F32 = mybir.dt.float32
BF16 = mybir.dt.bfloat16
AF = mybir.ActivationFunctionType
ALU = mybir.AluOpType
AX = mybir.AxisListType

D = 4096
S = 8192
NCORES = 8
TOK = 1024
CTX = 2048
H = 32
DH = 128
WIN = 1024
INTER = 11008
DC = D // 128
IC = INTER // 128
NQC = TOK // 128
NTILE = CTX // 128
ISCALE = 0.08838834764831845

ENGINES = ("sync", "scalar", "vector", "tensor", "gpsimd")


class _S:
    __slots__ = ("name", "n", "h")

    def __init__(self, name):
        self.name = name
        self.n = 0
        self.h = None


class _B:
    def __init__(self):
        self.ops = {e: [] for e in ENGINES}
        self.sems = []

    def sem(self, name):
        s = _S(name)
        self.sems.append(s)
        return s

    def op(self, eng, fn, waits=(), inc=None, inc_n=1):
        w = tuple((s, int(v)) for (s, v) in waits if v > 0)
        tgt = None
        if inc is not None:
            inc.n += inc_n
            tgt = (inc, inc_n)
        self.ops[eng].append((w, fn, tgt))
        return inc.n if inc is not None else 0

    def dma(self, q, out_ap, in_ap, sem, waits=()):
        return self.op(
            q,
            (lambda e, o=out_ap, i=in_ap: e.dma_start(out=o, in_=i)),
            waits, inc=sem, inc_n=16,
        )


def _realize(b, blk):
    for ename in ENGINES:
        ops = b.ops[ename]
        if not ops:
            continue

        def mk(ops):
            def body(eng):
                for waits, fn, inc in ops:
                    for s, v in waits:
                        eng.wait_ge(s.h, v)
                    ins = fn(eng)
                    if inc is not None:
                        ins.then_inc(inc[0].h, inc[1])
            return body

        getattr(blk, ename)(mk(ops))


def build_program(dbg=()):
    nc = bass.Bass("TRN2", target_bir_lowering=False)

    d_xtok = nc.dram_tensor("xtok", [CTX, D], F32, kind="ExternalInput").ap()
    d_xfm = nc.dram_tensor("xfm", [D, TOK], F32, kind="ExternalInput").ap()
    d_wq = nc.dram_tensor("wq", [DC, 128, D], F32, kind="ExternalInput").ap()
    d_wk = nc.dram_tensor("wk", [DC, 128, D], F32, kind="ExternalInput").ap()
    d_wv = nc.dram_tensor("wv", [DC, 128, D], F32, kind="ExternalInput").ap()
    d_wo = nc.dram_tensor("wo", [DC, 128, D], F32, kind="ExternalInput").ap()
    d_wg = nc.dram_tensor("wg", [IC, 128, D], F32, kind="ExternalInput").ap()
    d_wu = nc.dram_tensor("wu", [IC, 128, D], F32, kind="ExternalInput").ap()
    d_wd = nc.dram_tensor("wd", [DC, 128, INTER], F32, kind="ExternalInput").ap()
    d_bg = nc.dram_tensor("bg", [128, IC], F32, kind="ExternalInput").ap()
    d_bu = nc.dram_tensor("bu", [128, IC], F32, kind="ExternalInput").ap()
    d_bd = nc.dram_tensor("bd", [128, DC], F32, kind="ExternalInput").ap()
    d_cos = nc.dram_tensor("cosb", [128, CTX], F32, kind="ExternalInput").ap()
    d_sinr = nc.dram_tensor("sinr", [128, CTX], F32, kind="ExternalInput").ap()
    d_ident = nc.dram_tensor("ident", [128, 128], BF16, kind="ExternalInput").ap()
    d_rperm = nc.dram_tensor("rperm", [128, 128], BF16, kind="ExternalInput").ap()
    d_masks = nc.dram_tensor("masks", [128, NQC * 1152], BF16, kind="ExternalInput").ap()
    d_y = nc.dram_tensor("y", [D, TOK], F32, kind="ExternalOutput").ap()

    sk = "ExternalOutput" if "scratch" in dbg else "Internal"
    d_ksc = nc.dram_tensor("ksc", [H, 128, CTX], BF16, kind=sk).ap()
    d_vsc = nc.dram_tensor("vsc", [H, 128, CTX], BF16, kind=sk).ap()
    d_qsc = nc.dram_tensor("qsc", [H, 128, TOK], BF16, kind=sk).ap()
    d_xnsc = nc.dram_tensor("xnsc", [DC, 128, TOK], BF16, kind=sk).ap()
    d_wosc = nc.dram_tensor("wosc", [DC, 128, TOK], F32, kind=sk).ap()
    d_wdsc = nc.dram_tensor("wdsc", [DC, 128, INTER], BF16, kind="Internal").ap()
    d_attn = None
    if "attn" in dbg:
        d_attn = nc.dram_tensor("attnd", [128, H * TOK], BF16, kind="ExternalOutput").ap()

    b = _B()
    s_pe = b.sem("pe")
    s_dve = b.sem("dve")
    s_act = b.sem("act")
    s_gp = b.sem("gp")
    s_misc = b.sem("misc")
    s_xt = [b.sem(f"xt{i}") for i in range(2)]
    s_xnsp = b.sem("xnsp")
    s_wsl = [b.sem(f"wsl{i}") for i in range(2)]
    s_spl = [b.sem(f"spl{i}") for i in range(2)]
    s_kb = [b.sem(f"kb{i}") for i in range(4)]
    s_vb = [b.sem(f"vb{i}") for i in range(4)]
    s_qb = [b.sem(f"qb{i}") for i in range(4)]
    s_wob = [b.sem(f"wob{i}") for i in range(2)]
    s_wow = [b.sem(f"wow{i}") for i in range(2)]
    s_wdp = [b.sem(f"wdp{i}") for i in range(2)]
    s_wdw = [b.sem(f"wdw{i}") for i in range(2)]
    s_xnh = b.sem("xnh")
    s_wg = [b.sem(f"wg{i}") for i in range(2)]
    s_wu = [b.sem(f"wu{i}") for i in range(2)]
    s_wdb = [b.sem(f"wdb{i}") for i in range(2)]
    s_xf = [b.sem(f"xf{i}") for i in range(2)]
    s_wo2 = [b.sem(f"wo2{i}") for i in range(2)]
    s_y = [b.sem(f"y{i}") for i in range(2)]
    s_dbg = b.sem("dbg")

    root = contextlib.ExitStack()
    with root:
        t_cos = root.enter_context(nc.sbuf_tensor("t_cos", [128, CTX], F32))
        t_sinr = root.enter_context(nc.sbuf_tensor("t_sinr", [128, CTX], F32))
        t_id = root.enter_context(nc.sbuf_tensor("t_id", [128, 128], BF16))
        t_rp = root.enter_context(nc.sbuf_tensor("t_rp", [128, 128], BF16))
        t_bg = root.enter_context(nc.sbuf_tensor("t_bg", [128, IC], F32))
        t_bu = root.enter_context(nc.sbuf_tensor("t_bu", [128, IC], F32))
        t_bd = root.enter_context(nc.sbuf_tensor("t_bd", [128, DC], F32))

        t_eps = root.enter_context(nc.sbuf_tensor("t_eps", [128, 1], F32))
        b.op("vector", lambda e: e.memset(t_eps[:], 1e-5), inc=s_dve)

        b.dma("sync", t_cos[:], d_cos, s_misc)
        b.dma("sync", t_sinr[:], d_sinr, s_misc)
        b.dma("sync", t_id[:], d_ident, s_misc)
        b.dma("sync", t_rp[:], d_rperm, s_misc)
        b.dma("sync", t_bg[:], d_bg, s_misc)
        b.dma("sync", t_bu[:], d_bu, s_misc)
        b.dma("sync", t_bd[:], d_bd, s_misc)
        CONST_TOT = s_misc.n

        S1 = contextlib.ExitStack()
        t_xn = S1.enter_context(nc.sbuf_tensor("t_xn", [128, DC * CTX], BF16))

        P1 = contextlib.ExitStack()
        t_xt = [P1.enter_context(nc.sbuf_tensor(f"t_xt{i}", [128, D], F32)) for i in range(2)]
        t_xnt = [P1.enter_context(nc.sbuf_tensor(f"t_xnt{i}", [128, D], BF16)) for i in range(2)]
        t_sum = [P1.enter_context(nc.sbuf_tensor(f"t_sum{i}", [128, 1], F32)) for i in range(2)]
        t_sq = [P1.enter_context(nc.sbuf_tensor(f"t_sq{i}", [128, 1], F32)) for i in range(2)]
        t_mean = [P1.enter_context(nc.sbuf_tensor(f"t_mean{i}", [128, 1], F32)) for i in range(2)]
        t_var = [P1.enter_context(nc.sbuf_tensor(f"t_var{i}", [128, 1], F32)) for i in range(2)]
        t_rstd = [P1.enter_context(nc.sbuf_tensor(f"t_rstd{i}", [128, 1], F32)) for i in range(2)]
        t_nb = [P1.enter_context(nc.sbuf_tensor(f"t_nb{i}", [128, 1], F32)) for i in range(2)]
        p_tr = [P1.enter_context(nc.psum_tensor(f"p_tr{i}", [128, 1024], BF16)) for i in range(2)]

        xtok_r = d_xtok.rearrange("(n p) d -> n p d", p=128)
        xn_r3 = t_xn[:].rearrange("p (dc t) -> p dc t", dc=DC)

        norm_cnt = [0, 0]
        red_cnt = [0, 0]
        trg_cnt = {}
        evac_cnt = {}
        sqw_cnt = [0, 0]
        m2w_cnt = [0, 0]
        lnr_cnt = [0, 0]
        nbw_cnt = [0, 0]
        pn_free = {}

        for i in range(NTILE):
            sl = i % 2
            w = [(s_misc, CONST_TOT)]
            if i >= 2:
                w += [(s_act, norm_cnt[sl]), (s_dve, red_cnt[sl])]
            fills = b.dma("sync", t_xt[sl][:], xtok_r[i], s_xt[sl], waits=w)
            red_cnt[sl] = b.op(
                "vector",
                lambda e, o=t_sum[sl], x=t_xt[sl]: e.reduce_sum(o[:], x[:], axis=AX.X),
                waits=[(s_xt[sl], fills), (s_dve, m2w_cnt[sl])],
                inc=s_dve)
            w = [(s_xt[sl], fills), (s_dve, sqw_cnt[sl])]
            if i >= 2:
                w.append((s_pe, trg_cnt[(i - 2, 3)]))
            sq_cnt = b.op(
                "scalar",
                lambda e, o=t_xnt[sl], x=t_xt[sl], a=t_sq[sl]: e.activation(
                    o[:], x[:], AF.Square, accum_out=a[:]),
                waits=w, inc=s_act)
            sqw_cnt[sl] = b.op(
                "vector",
                lambda e, o=t_var[sl], x=t_sq[sl]: e.tensor_scalar_mul(o[:], x[:], 1.0 / D),
                waits=[(s_act, sq_cnt), (s_act, lnr_cnt[sl])], inc=s_dve)
            mean_cnt = b.op(
                "vector",
                lambda e, o=t_mean[sl], x=t_sum[sl]: e.tensor_scalar_mul(o[:], x[:], 1.0 / D),
                waits=[(s_dve, red_cnt[sl])], inc=s_dve)
            m2_cnt = b.op(
                "vector",
                lambda e, o=t_sum[sl], m=t_mean[sl]: e.tensor_mul(o[:], m[:], m[:]),
                waits=[(s_dve, mean_cnt)], inc=s_dve)
            m2w_cnt[sl] = m2_cnt
            var_cnt = b.op(
                "vector",
                lambda e, o=t_var[sl], m2=t_sum[sl]: e.tensor_sub(o[:], o[:], m2[:]),
                waits=[(s_dve, m2_cnt), (s_dve, sqw_cnt[sl])], inc=s_dve)
            w = [(s_dve, var_cnt), (s_dve, nbw_cnt[sl])]
            ln_cnt = b.op(
                "scalar",
                lambda e, o=t_rstd[sl], v=t_var[sl]: e.activation(o[:], v[:], AF.Ln, bias=t_eps[:]),
                waits=w, inc=s_act)
            lnr_cnt[sl] = b.op(
                "scalar",
                lambda e, o=t_rstd[sl]: e.activation(o[:], o[:], AF.Exp, scale=-0.5),
                waits=[(s_act, ln_cnt)], inc=s_act)
            w = [(s_act, lnr_cnt[sl])]
            if i >= 2:
                w.append((s_act, norm_cnt[sl]))
            w.append((s_dve, mean_cnt))
            nbw_cnt[sl] = b.op(
                "vector",
                lambda e, o=t_nb[sl], m=t_mean[sl], r=t_rstd[sl]: e.scalar_tensor_tensor(
                    o[:], m[:], -1.0, r[:], ALU.mult, ALU.mult),
                waits=w, inc=s_dve)
            norm_cnt[sl] = b.op(
                "scalar",
                lambda e, o=t_xnt[sl], x=t_xt[sl], r=t_rstd[sl], nb=t_nb[sl]: e.activation(
                    o[:], x[:], AF.Identity, bias=nb[:], scale=r[:]),
                waits=[(s_dve, nbw_cnt[sl]), (s_act, lnr_cnt[sl])], inc=s_act)
            for g in range(4):
                pg = p_tr[g % 2]
                w = [(s_act, norm_cnt[sl]), (s_misc, CONST_TOT)]
                key = (i, g - 2) if g >= 2 else (i - 1, g + 2)
                if key in evac_cnt:
                    w.append((s_dve, evac_cnt[key]))

                def tr_group(e, pg=pg, src=t_xnt[sl], g=g):
                    ins = None
                    for j in range(8):
                        ins = e.transpose(
                            pg[:, j * 128:(j + 1) * 128],
                            src[:, (g * 8 + j) * 128:(g * 8 + j + 1) * 128],
                            t_id[:])
                    return ins
                trg_cnt[(i, g)] = b.op("tensor", tr_group, waits=w, inc=s_pe)
                dst = xn_r3[:, g * 8:(g + 1) * 8, i * 128:(i + 1) * 128]
                src = pg[:].rearrange("p (a c) -> p a c", a=8)
                evac_cnt[(i, g)] = b.op(
                    "vector",
                    lambda e, d=dst, s=src: e.tensor_copy(d, s),
                    waits=[(s_pe, trg_cnt[(i, g)])], inc=s_dve)

        DVE_P1 = s_dve.n
        ACT_P1 = s_act.n
        PE_P1 = s_pe.n
        last_evac = s_dve.n
        for dc in range(DC):
            b.dma("sync", d_xnsc[dc],
                  t_xn[:, dc * CTX + TOK: dc * CTX + CTX],
                  s_xnsp, waits=[(s_dve, last_evac)] if dc == 0 else ())
        P1.close()

        P2a = contextlib.ExitStack()
        t_wsl = [P2a.enter_context(nc.sbuf_tensor(f"t_wsl{i}", [128, D], BF16)) for i in range(2)]
        t_ksb = [P2a.enter_context(nc.sbuf_tensor(f"t_ksb{i}", [128, 512], BF16)) for i in range(3)]
        t_t1 = [P2a.enter_context(nc.sbuf_tensor(f"t_t1{i}", [128, 512], F32)) for i in range(3)]
        t_t2 = [P2a.enter_context(nc.sbuf_tensor(f"t_t2{i}", [128, 512], F32)) for i in range(3)]
        t_ko = [P2a.enter_context(nc.sbuf_tensor(f"t_ko{i}", [128, CTX], BF16)) for i in range(2)]
        p_pk = [P2a.enter_context(nc.psum_tensor(f"p_pk{i}", [128, 512], F32)) for i in range(3)]
        p_rt = [P2a.enter_context(nc.psum_tensor(f"p_rt{i}", [128, 512], F32)) for i in range(3)]

        wsl_rel = [0, 0]
        wsl_fill = [0, 0]
        ko_rel = [(None, 0), (None, 0)]
        pk_rel = [(0, 0)] * 3
        rt_rel = [0, 0, 0]
        t12_rel = [0, 0, 0]
        ksb_rel = [(0, 0)] * 3
        uid = [0]

        def qkv_pass(kind):
            wd = {"k": d_wk, "v": d_wv, "q": d_wq}[kind]
            out_sc = {"k": d_ksc, "v": d_vsc, "q": d_qsc}[kind]
            ntt = 2 if kind == "q" else 4
            c0 = TOK if kind == "q" else 0
            rope = kind != "v"
            for h in range(H):
                sl = h % 2
                w = [(s_pe, wsl_rel[sl])]
                if kind == "k" and h < 2:
                    w += [(s_act, ACT_P1), (s_pe, PE_P1), (s_dve, DVE_P1),
                          (s_xnsp, 16 * DC)]
                wsl_fill[sl] = b.dma("gpsimd", t_wsl[sl][:], wd[h], s_wsl[sl], waits=w)
                npiece = 0
                for tt in range(ntt):
                    u = uid[0]
                    uid[0] += 1
                    pb = u % 3
                    pk = p_pk[pb]
                    w = [(s_wsl[sl], wsl_fill[sl]),
                         (s_act, pk_rel[pb][0]), (s_dve, pk_rel[pb][1])]
                    if kind == "k" and h == 0 and tt == 0:
                        w.append((s_dve, DVE_P1))

                    def proj(e, pk=pk, wb=t_wsl[sl], c0=c0, tt=tt):
                        ins = None
                        for dc in range(DC):
                            ins = e.matmul(
                                pk[:],
                                wb[:, dc * 128:(dc + 1) * 128],
                                t_xn[:, dc * CTX + c0 + tt * 512: dc * CTX + c0 + (tt + 1) * 512],
                                start=(dc == 0), stop=(dc == DC - 1))
                        return ins
                    proj_cnt = b.op("tensor", proj, waits=w, inc=s_pe)
                    if rope:
                        ev = b.op(
                            "scalar",
                            lambda e, o=t_ksb[pb], p=pk: e.activation(o[:], p[:], AF.Identity),
                            waits=[(s_pe, proj_cnt), (s_pe, ksb_rel[pb][0]),
                                   (s_dve, ksb_rel[pb][1])], inc=s_act)
                        rot_cnt = b.op(
                            "tensor",
                            lambda e, o=p_rt[pb], i=t_ksb[pb]: e.matmul(
                                o[:], t_rp[:], i[:], start=True, stop=True),
                            waits=[(s_act, ev), (s_dve, rt_rel[pb])], inc=s_pe)
                        ksb_rel[pb] = (rot_cnt, 0)
                        cs = c0 + tt * 512
                        cos_cnt = b.op(
                            "vector",
                            lambda e, o=t_t1[pb], p=t_ksb[pb], cs=cs: e.tensor_mul(
                                o[:], p[:], t_cos[:, cs:cs + 512]),
                            waits=[(s_act, ev), (s_dve, t12_rel[pb])], inc=s_dve)
                        sin_cnt = b.op(
                            "vector",
                            lambda e, o=t_t2[pb], p=p_rt[pb], cs=cs: e.tensor_mul(
                                o[:], p[:], t_sinr[:, cs:cs + 512]),
                            waits=[(s_pe, rot_cnt)], inc=s_dve)
                        rt_rel[pb] = sin_cnt
                        ksb_rel[pb] = (rot_cnt, cos_cnt)
                        w2 = [(s_dve, sin_cnt)]
                        if tt == 0 and ko_rel[sl][0] is not None:
                            w2.append(ko_rel[sl])
                        piece = b.op(
                            "vector",
                            lambda e, o=t_ko[sl], a=t_t1[pb], c=t_t2[pb], tt=tt: e.tensor_add(
                                o[:, tt * 512:(tt + 1) * 512], a[:], c[:]),
                            waits=w2, inc=s_dve)
                        t12_rel[pb] = piece
                        npiece = piece
                        pk_rel[pb] = (ev, 0)
                        piece_sem = s_dve
                    else:
                        w2 = [(s_pe, proj_cnt)]
                        if tt == 0 and ko_rel[sl][0] is not None:
                            w2.append(ko_rel[sl])
                        piece = b.op(
                            "scalar",
                            lambda e, o=t_ko[sl], p=pk, tt=tt: e.activation(
                                o[:, tt * 512:(tt + 1) * 512], p[:], AF.Identity),
                            waits=w2, inc=s_act)
                        npiece = piece
                        pk_rel[pb] = (piece, pk_rel[pb][1])
                        piece_sem = s_act
                wsl_rel[sl] = proj_cnt
                spl_f = b.dma("sync", out_sc[h], t_ko[sl][:, 0:ntt * 512],
                              s_spl[sl], waits=[(piece_sem, npiece)])
                ko_rel[sl] = (s_spl[sl], spl_f)

        qkv_pass("k")
        qkv_pass("v")
        qkv_pass("q")

        PE_2A = s_pe.n
        DVE_2A = s_dve.n
        ACT_2A = s_act.n
        SPL_TOT = (s_spl[0].n, s_spl[1].n)
        P2a.close()
        S1.close()

        S2 = contextlib.ExitStack()
        t_att = S2.enter_context(nc.sbuf_tensor("t_att", [128, H * TOK], BF16))

        P2b = contextlib.ExitStack()
        t_msk = P2b.enter_context(nc.sbuf_tensor("t_msk", [128, NQC * 1152], BF16))
        t_kb = [P2b.enter_context(nc.sbuf_tensor(f"t_kb{i}", [128, CTX], BF16)) for i in range(4)]
        t_vb = [P2b.enter_context(nc.sbuf_tensor(f"t_vb{i}", [128, CTX], BF16)) for i in range(4)]
        t_qb = [P2b.enter_context(nc.sbuf_tensor(f"t_qb{i}", [128, TOK], BF16)) for i in range(4)]
        t_vkm = [P2b.enter_context(nc.sbuf_tensor(f"t_vkm{i}", [128, CTX], BF16)) for i in range(2)]
        t_ssb = [P2b.enter_context(nc.sbuf_tensor(f"t_ssb{i}", [128, 1152], F32)) for i in range(2)]
        t_pn = [P2b.enter_context(nc.sbuf_tensor(f"t_pn{i}", [128, 1152], BF16)) for i in range(4)]
        t_pT = [P2b.enter_context(nc.sbuf_tensor(f"t_pT{i}", [128, 1152], BF16)) for i in range(2)]
        t_nm = [P2b.enter_context(nc.sbuf_tensor(f"t_nm{i}", [128, 1], F32)) for i in range(2)]
        t_su = [P2b.enter_context(nc.sbuf_tensor(f"t_su{i}", [128, 1], F32)) for i in range(2)]
        t_rc = [P2b.enter_context(nc.sbuf_tensor(f"t_rc{i}", [128, 1], F32)) for i in range(2)]
        t_wdp = [P2b.enter_context(nc.sbuf_tensor(f"t_wdp{i}", [128, 5504], BF16)) for i in range(2)]
        p_sc = P2b.enter_context(nc.psum_tensor("p_sc", [128, 1152], F32))
        p_pT = P2b.enter_context(nc.psum_tensor("p_pT", [128, 1152], BF16))
        p_av = P2b.enter_context(nc.psum_tensor("p_av", [128, 128], F32))
        p_vt = [P2b.enter_context(nc.psum_tensor(f"p_vt{i}", [128, 1024], BF16)) for i in range(2)]

        bar2b = [(s_pe, PE_2A), (s_dve, DVE_2A), (s_act, ACT_2A),
                 (s_spl[0], SPL_TOT[0]), (s_spl[1], SPL_TOT[1])]
        b.dma("sync", t_msk[:], d_masks, s_misc, waits=bar2b)
        MSK_TOT = s_misc.n

        wdp_rel = [(None, 0), (None, 0)]
        for u2 in range(2 * DC):
            fc, hf = u2 // 2, u2 % 2
            sl = u2 % 2
            w = list(bar2b) if u2 < 2 else []
            if wdp_rel[sl][0] is not None:
                w.append(wdp_rel[sl])
            f = b.dma("gpsimd", t_wdp[sl][:],
                      d_wd[fc][:, hf * 5504:(hf + 1) * 5504], s_wdp[sl], waits=w)
            fw = b.dma("gpsimd", d_wdsc[fc][:, hf * 5504:(hf + 1) * 5504],
                       t_wdp[sl][:], s_wdw[sl], waits=[(s_wdp[sl], f)])
            wdp_rel[sl] = (s_wdw[sl], fw)
        WDW_TOT = (s_wdw[0].n, s_wdw[1].n)

        kb_rel = [0, 0, 0, 0]
        vb_rel = [0, 0, 0, 0]
        qb_rel = [0, 0, 0, 0]
        vkm_rel = [0, 0]
        vt_rel = [0, 0]
        sc_rel = 0
        ssb_rel = [0, 0]
        pT_rel = 0
        pTs_rel = [0, 0]
        pn_rel = [0, 0, 0, 0]
        av_rel = 0
        nm_rel = [0, 0]
        su_rel = [0, 0]
        rc_rel = [0, 0]
        av_done = {}
        pT_ev = {}

        for hp in range(H // 2):
            heads = (2 * hp, 2 * hp + 1)
            kbf = {}
            vbf = {}
            qbf = {}
            for h in heads:
                sl = h % 4
                w = list(bar2b) if hp == 0 else [(s_pe, kb_rel[sl])]
                kbf[h] = b.dma("sync", t_kb[sl][:], d_ksc[h], s_kb[sl], waits=w)
                w = list(bar2b) if hp == 0 else [(s_pe, vb_rel[sl])]
                vbf[h] = b.dma("sync", t_vb[sl][:], d_vsc[h], s_vb[sl], waits=w)
                w = list(bar2b) if hp == 0 else [(s_pe, qb_rel[sl])]
                qbf[h] = b.dma("sync", t_qb[sl][:], d_qsc[h], s_qb[sl], waits=w)

            vkm_cnt = {}
            for hi, h in enumerate(heads):
                sl = h % 4
                for g in range(2):
                    w = [(s_vb[sl], vbf[h]), (s_misc, CONST_TOT),
                         (s_act, vt_rel[g]), (s_pe, vkm_rel[hi])]

                    def vtr(e, g=g, vb=t_vb[sl]):
                        ins = None
                        for j2 in range(8):
                            kc = g * 8 + j2
                            ins = e.transpose(
                                p_vt[g][:, j2 * 128:(j2 + 1) * 128],
                                vb[:, kc * 128:(kc + 1) * 128], t_id[:])
                        return ins
                    c = b.op("tensor", vtr, waits=w, inc=s_pe)
                    vt_rel[g] = b.op(
                        "scalar",
                        lambda e, g=g, vk=t_vkm[hi]: e.activation(
                            vk[:, g * 1024:(g + 1) * 1024], p_vt[g][:], AF.Identity),
                        waits=[(s_pe, c)], inc=s_act)
                vkm_cnt[h] = vt_rel[1]
                vb_rel[sl] = c

            pn_cnt = {}
            exp_cnt = {}
            rc_cnt = {}
            NJ = 2 * NQC
            for j in range(NJ + 3):
                if 2 <= j <= NJ + 1:
                    jb = j - 2
                    q4 = jb % 4
                    q2 = jb % 2
                    w = [(s_act, exp_cnt[jb]), (s_act, pT_rel)]

                    def ptr(e, pn=t_pn[q4]):
                        ins = None
                        for j2 in range(9):
                            ins = e.transpose(
                                p_pT[:, j2 * 128:(j2 + 1) * 128],
                                pn[:, j2 * 128:(j2 + 1) * 128], t_id[:])
                        return ins
                    pn_rel[q4] = b.op("tensor", ptr, waits=w, inc=s_pe)
                    pT_rel = b.op(
                        "scalar",
                        lambda e, pT=t_pT[q2]: e.activation(pT[:], p_pT[:], AF.Identity),
                        waits=[(s_pe, pn_rel[q4]), (s_pe, pTs_rel[q2])], inc=s_act)
                    pTev_cnt = {}
                    pTev_cnt[jb] = pT_rel
                    pT_ev[jb] = pT_rel
                if j < NJ:
                    hi = j % 2
                    h = heads[hi]
                    qc = j // 2
                    sl = h % 4
                    q2 = j % 2
                    q4 = j % 4
                    w = [(s_qb[sl], qbf[h]), (s_kb[sl], kbf[h]), (s_dve, sc_rel)]

                    def scores(e, qb=t_qb[sl], kb=t_kb[sl], qc=qc):
                        base = qc * 128
                        ins = None
                        for off, wd in ((0, 512), (512, 512), (1024, 128)):
                            ins = e.matmul(
                                p_sc[:, off:off + wd],
                                qb[:, qc * 128:(qc + 1) * 128],
                                kb[:, base + off:base + off + wd],
                                start=True, stop=True)
                        return ins
                    sc_cnt = b.op("tensor", scores, waits=w, inc=s_pe)
                    w = [(s_pe, sc_cnt), (s_misc, MSK_TOT), (s_act, ssb_rel[q2])]

                    def mask3(e, qc=qc, ss=t_ssb[q2]):
                        ins = None
                        for off, wd in ((0, 512), (512, 512), (1024, 128)):
                            ins = e.scalar_tensor_tensor(
                                ss[:, off:off + wd],
                                p_sc[:, off:off + wd],
                                ISCALE,
                                t_msk[:, qc * 1152 + off: qc * 1152 + off + wd],
                                ALU.mult, ALU.add)
                        return ins
                    sc_rel = b.op("vector", mask3, waits=w, inc=s_dve)
                    negm_cnt = b.op(
                        "vector",
                        lambda e, o=t_nm[q2], ss=t_ssb[q2]: e.reduce_max(
                            o[:], ss[:], axis=AX.X, negate=True),
                        waits=[(s_act, nm_rel[q2]), (s_dve, sc_rel)], inc=s_dve)
                    w = [(s_dve, negm_cnt), (s_dve, su_rel[q2]),
                         (s_pe, pn_rel[q4])]
                    exp_cnt[j] = b.op(
                        "scalar",
                        lambda e, nb=t_nm[q2], su=t_su[q2], pn=t_pn[q4],
                        ss=t_ssb[q2]: e.activation(
                            pn[:], ss[:], AF.Exp, bias=nb[:], accum_out=su[:]),
                        waits=w, inc=s_act)
                    ssb_rel[q2] = exp_cnt[j]
                    nm_rel[q2] = exp_cnt[j]
                if 3 <= j:
                    jc = j - 3
                    hi = jc % 2
                    h = heads[hi]
                    qc = jc // 2
                    q2 = jc % 2
                    w = [(s_act, pT_ev[jc]), (s_act, av_rel)]
                    if qc == 0:
                        w.append((s_act, vkm_cnt[h]))

                    def av(e, qc=qc, pT=t_pT[q2], vk=t_vkm[hi]):
                        ins = None
                        for j2 in range(9):
                            kc = qc + j2
                            ins = e.matmul(
                                p_av[:],
                                pT[:, j2 * 128:(j2 + 1) * 128],
                                vk[:, kc * 128:(kc + 1) * 128],
                                start=(j2 == 0), stop=(j2 == 8))
                        return ins
                    pTs_rel[q2] = b.op("tensor", av, waits=w, inc=s_pe)
                    av_done[h] = pTs_rel[q2]
                    w = [(s_pe, pTs_rel[q2]), (s_dve, rc_cnt[jc])]
                    av_rel = b.op(
                        "scalar",
                        lambda e, r=t_rc[q2], h=h, qc=qc: e.activation(
                            t_att[:, qc * 4096 + h * 128: qc * 4096 + (h + 1) * 128],
                            p_av[:], AF.Identity, scale=r[:]),
                        waits=w, inc=s_act)
                    rc_rel[q2] = av_rel
                if 1 <= j <= NJ:
                    jr = j - 1
                    q2 = jr % 2
                    w = [(s_act, exp_cnt[jr]), (s_act, rc_rel[q2])]
                    su_rel[q2] = b.op(
                        "vector",
                        lambda e, o=t_rc[q2], i=t_su[q2]: e.reciprocal(o[:], i[:]),
                        waits=w, inc=s_dve)
                    rc_cnt[jr] = su_rel[q2]
            for hi, h in enumerate(heads):
                sl = h % 4
                kb_rel[sl] = av_done[h]
                qb_rel[sl] = av_done[h]
                vkm_rel[hi] = av_done[h]

        PE_2B = s_pe.n
        DVE_2B = s_dve.n
        ACT_2B = s_act.n
        if d_attn is not None:
            b.dma("sync", d_attn, t_att[:], s_dbg, waits=[(s_act, ACT_2B)])
        P2b.close()

        P3 = contextlib.ExitStack()
        t_wob = [P3.enter_context(nc.sbuf_tensor(f"t_wob{i}", [128, D], BF16)) for i in range(2)]
        t_wot = [P3.enter_context(nc.sbuf_tensor(f"t_wot{i}", [128, 512], F32)) for i in range(2)]
        p_wo = [P3.enter_context(nc.psum_tensor(f"p_wo{i}", [128, 512], F32)) for i in range(2)]

        bar3 = [(s_pe, PE_2B), (s_dve, DVE_2B), (s_act, ACT_2B)]
        t_afm = P3.enter_context(nc.sbuf_tensor("t_afm", [128, H * TOK], BF16))
        p_at = [P3.enter_context(nc.psum_tensor(f"p_at{i}", [128, 1024], BF16)) for i in range(2)]
        at_ev = [0, 0]
        for u3 in range(32):
            qc3, hg = u3 // 4, u3 % 4
            w = list(bar3) + [(s_misc, CONST_TOT), (s_dve, at_ev[u3 % 2])]

            def atr(e, qc3=qc3, hg=hg, pg=p_at[u3 % 2]):
                ins = None
                for j3 in range(8):
                    h3 = hg * 8 + j3
                    ins = e.transpose(
                        pg[:, j3 * 128:(j3 + 1) * 128],
                        t_att[:, qc3 * 4096 + h3 * 128: qc3 * 4096 + (h3 + 1) * 128],
                        t_id[:])
                return ins
            c3 = b.op("tensor", atr, waits=w, inc=s_pe)
            dst = t_afm[:].rearrange("p (h t) -> p h t", h=H)[
                :, hg * 8:(hg + 1) * 8, qc3 * 128:(qc3 + 1) * 128]
            at_ev[u3 % 2] = b.op(
                "vector",
                lambda e, d=dst, pg=p_at[u3 % 2]: e.tensor_copy(
                    d, pg[:].rearrange("p (a c) -> p a c", a=8)),
                waits=[(s_pe, c3)], inc=s_dve)
        ATT_FM_DONE = s_dve.n
        wob_rel = [0, 0]
        wot_rel = [(None, 0), (None, 0)]
        pwo_rel = [0, 0]
        for fc in range(DC):
            sl = fc % 2
            w = list(bar3) if fc < 2 else [(s_pe, wob_rel[sl])]
            wobf = b.dma("gpsimd", t_wob[sl][:], d_wo[fc], s_wob[sl], waits=w)
            for tt in range(2):
                u = fc * 2 + tt
                pb = u % 2
                w = [(s_wob[sl], wobf), (s_act, pwo_rel[pb])]
                if fc == 0 and tt == 0:
                    w.append((s_dve, ATT_FM_DONE))

                def womm(e, wb=t_wob[sl], tt=tt, pb=pb):
                    ins = None
                    for dc in range(DC):
                        ins = e.matmul(
                            p_wo[pb][:],
                            wb[:, dc * 128:(dc + 1) * 128],
                            t_afm[:, dc * TOK + tt * 512: dc * TOK + (tt + 1) * 512],
                            start=(dc == 0), stop=(dc == DC - 1))
                    return ins
                mmc = b.op("tensor", womm, waits=w, inc=s_pe)
                w = [(s_pe, mmc)]
                if wot_rel[pb][0] is not None:
                    w.append(wot_rel[pb])
                pwo_rel[pb] = b.op(
                    "scalar",
                    lambda e, o=t_wot[pb], p=p_wo[pb]: e.activation(o[:], p[:], AF.Identity),
                    waits=w, inc=s_act)
                wf = b.dma("sync", d_wosc[fc][:, tt * 512:(tt + 1) * 512],
                           t_wot[pb][:], s_wow[pb], waits=[(s_act, pwo_rel[pb])])
                wot_rel[pb] = (s_wow[pb], wf)
            wob_rel[sl] = mmc

        PE_P3 = s_pe.n
        ACT_P3 = s_act.n
        WOW_TOT = (s_wow[0].n, s_wow[1].n)
        P3.close()
        S2.close()

        S4 = contextlib.ExitStack()
        t_h = S4.enter_context(nc.sbuf_tensor("t_h", [128, IC * 512], BF16))

        bar4 = [(s_pe, PE_P3), (s_act, ACT_P3),
                (s_wow[0], WOW_TOT[0]), (s_wow[1], WOW_TOT[1]),
                (s_wdw[0], WDW_TOT[0]), (s_wdw[1], WDW_TOT[1])]
        if s_dbg.n:
            bar4.append((s_dbg, s_dbg.n))
        gate_tot_pe = 0
        down_tot = {"pe": 0, "dve": 0, "y0": 0, "y1": 0}
        wdb_rel = [0, 0]
        for tt in range(2):
            P4a = contextlib.ExitStack()
            t_xnh = P4a.enter_context(nc.sbuf_tensor(f"t_xnh{tt}", [128, DC * 512], BF16))
            t_wgb = [P4a.enter_context(nc.sbuf_tensor(f"t_wgb{tt}_{i}", [128, D], BF16)) for i in range(2)]
            t_wub = [P4a.enter_context(nc.sbuf_tensor(f"t_wub{tt}_{i}", [128, D], BF16)) for i in range(2)]
            t_gs = [P4a.enter_context(nc.sbuf_tensor(f"t_gs{tt}_{i}", [128, 512], BF16)) for i in range(2)]
            t_us = [P4a.enter_context(nc.sbuf_tensor(f"t_us{tt}_{i}", [128, 512], BF16)) for i in range(2)]
            p_g = [P4a.enter_context(nc.psum_tensor(f"p_g{tt}_{i}", [128, 512], F32)) for i in range(2)]
            p_u = [P4a.enter_context(nc.psum_tensor(f"p_u{tt}_{i}", [128, 512], F32)) for i in range(2)]

            if tt == 0:
                barA = bar4
            else:
                barA = [(s_pe, down_tot["pe"]), (s_dve, down_tot["dve"]),
                        (s_y[0], down_tot["y0"]), (s_y[1], down_tot["y1"])]
            for dc in range(DC):
                w = list(barA) + [(s_xnsp, 16 * DC)] if dc == 0 else []
                if tt == 1 and dc == 0:
                    w.append((s_pe, gate_tot_pe))
                b.dma("sync", t_xnh[:, dc * 512:(dc + 1) * 512],
                      d_xnsc[dc][:, tt * 512:(tt + 1) * 512], s_xnh, waits=w)
            XNH_TOT = s_xnh.n

            wg_fill = [0, 0]
            wu_fill = [0, 0]
            wg_rel = [0, 0]
            gs_rel = [0, 0]
            pg_rel = [0, 0]
            pu_rel = [0, 0]
            mul_cnt = {}

            def load_wgu(ic):
                sl = ic % 2
                if ic < 2:
                    w = list(barA)
                else:
                    w = [(s_pe, wg_rel[sl])]
                wg_fill[sl] = b.dma("gpsimd", t_wgb[sl][:], d_wg[ic], s_wg[sl], waits=list(w))
                wu_fill[sl] = b.dma("gpsimd", t_wub[sl][:], d_wu[ic], s_wu[sl], waits=list(w))

            load_wgu(0)
            load_wgu(1)
            for ic in range(IC):
                sl = ic % 2
                w = [(s_wg[sl], wg_fill[sl]), (s_xnh, XNH_TOT),
                     (s_act, pg_rel[sl])]

                def gmm(e, wb=t_wgb[sl], pb=p_g[sl], xh=t_xnh):
                    ins = None
                    for dc in range(DC):
                        ins = e.matmul(
                            pb[:], wb[:, dc * 128:(dc + 1) * 128],
                            xh[:, dc * 512:(dc + 1) * 512],
                            start=(dc == 0), stop=(dc == DC - 1))
                    return ins
                gc = b.op("tensor", gmm, waits=w, inc=s_pe)
                w = [(s_wu[sl], wu_fill[sl]), (s_act, pu_rel[sl])]

                def umm(e, wb=t_wub[sl], pb=p_u[sl], xh=t_xnh):
                    ins = None
                    for dc in range(DC):
                        ins = e.matmul(
                            pb[:], wb[:, dc * 128:(dc + 1) * 128],
                            xh[:, dc * 512:(dc + 1) * 512],
                            start=(dc == 0), stop=(dc == DC - 1))
                    return ins
                uc = b.op("tensor", umm, waits=w, inc=s_pe)
                wg_rel[sl] = uc
                w = [(s_pe, gc), (s_gp, gs_rel[sl])]
                pg_rel[sl] = b.op(
                    "scalar",
                    lambda e, o=t_gs[sl], p=p_g[sl], ic=ic: e.activation(
                        o[:], p[:], AF.Silu, bias=t_bg[:, ic:ic + 1]),
                    waits=w, inc=s_act)
                pu_rel[sl] = b.op(
                    "scalar",
                    lambda e, o=t_us[sl], p=p_u[sl], ic=ic: e.activation(
                        o[:], p[:], AF.Identity, bias=t_bu[:, ic:ic + 1]),
                    waits=[(s_pe, uc)], inc=s_act)
                if ic + 2 < IC:
                    load_wgu(ic + 2)
                w = [(s_act, pu_rel[sl])]
                if tt == 1 and ic == 0:
                    w.append((s_pe, down_tot["pe"]))
                mul_cnt[ic] = b.op(
                    "gpsimd",
                    lambda e, g=t_gs[sl], u=t_us[sl], ic=ic: e.tensor_tensor(
                        t_h[:, ic * 512:(ic + 1) * 512], g[:], u[:], ALU.mult),
                    waits=w, inc=s_gp)
                gs_rel[sl] = mul_cnt[ic]
            gate_tot_pe = s_pe.n
            GATE_ACT = s_act.n
            MUL_TOT = s_gp.n
            P4a.close()

            P4b = contextlib.ExitStack()
            t_wdb = [P4b.enter_context(nc.sbuf_tensor(f"t_wdb{tt}_{i}", [128, INTER], BF16)) for i in range(2)]
            t_xf = [P4b.enter_context(nc.sbuf_tensor(f"t_xf{tt}_{i}", [128, 512], F32)) for i in range(2)]
            t_w2 = [P4b.enter_context(nc.sbuf_tensor(f"t_w2{tt}_{i}", [128, 512], F32)) for i in range(2)]
            t_ob = [P4b.enter_context(nc.sbuf_tensor(f"t_ob{tt}_{i}", [128, 512], F32)) for i in range(2)]
            p_d = [P4b.enter_context(nc.psum_tensor(f"p_d{tt}_{i}", [128, 512], F32)) for i in range(2)]

            barB = [(s_pe, gate_tot_pe), (s_act, GATE_ACT), (s_gp, MUL_TOT)]
            xf_rel = [0, 0]
            ob_rel = [(None, 0), (None, 0)]
            pd_rel = [0, 0]
            for fc in range(DC):
                sl = fc % 2
                w = [(s_pe, wdb_rel[sl])]
                if fc < 2:
                    w = list(barB) + [(s_pe, wdb_rel[sl])]
                    if tt == 0:
                        w += [(s_wdw[0], WDW_TOT[0]), (s_wdw[1], WDW_TOT[1])]
                wdbf = b.dma("sync", t_wdb[sl][:], d_wdsc[fc], s_wdb[sl], waits=w)
                w = [(s_dve, xf_rel[sl])] + (list(barB) if fc < 2 else [])
                xff = b.dma("scalar", t_xf[sl][:],
                            d_xfm[fc * 128:(fc + 1) * 128, tt * 512:(tt + 1) * 512],
                            s_xf[sl], waits=w)
                w = [(s_dve, xf_rel[sl])] + (list(barB) if fc < 2 else [])
                if tt == 0 and fc < 2:
                    w += [(s_wow[0], WOW_TOT[0]), (s_wow[1], WOW_TOT[1])]
                w2f = b.dma("scalar", t_w2[sl][:],
                            d_wosc[fc][:, tt * 512:(tt + 1) * 512], s_wo2[sl], waits=w)
                w = [(s_wdb[sl], wdbf), (s_gp, MUL_TOT), (s_dve, pd_rel[sl])]

                def dmm(e, wb=t_wdb[sl], pb=p_d[sl]):
                    ins = None
                    for ic in range(IC):
                        ins = e.matmul(
                            pb[:], wb[:, ic * 128:(ic + 1) * 128],
                            t_h[:, ic * 512:(ic + 1) * 512],
                            start=(ic == 0), stop=(ic == IC - 1))
                    return ins
                dc_cnt = b.op("tensor", dmm, waits=w, inc=s_pe)
                w = [(s_pe, dc_cnt), (s_xf[sl], xff)]
                if ob_rel[sl][0] is not None:
                    w.append(ob_rel[sl])
                o1_cnt = b.op(
                    "vector",
                    lambda e, o=t_ob[sl], p=p_d[sl], x=t_xf[sl], fc=fc: e.scalar_tensor_tensor(
                        o[:], p[:], t_bd[:, fc:fc + 1], x[:], ALU.add, ALU.add),
                    waits=w, inc=s_dve)
                xf_rel[sl] = b.op(
                    "vector",
                    lambda e, o=t_ob[sl], w2=t_w2[sl]: e.tensor_add(o[:], o[:], w2[:]),
                    waits=[(s_wo2[sl], w2f), (s_dve, o1_cnt)], inc=s_dve)
                pd_rel[sl] = xf_rel[sl]
                yf = b.dma("scalar", d_y[fc * 128:(fc + 1) * 128, tt * 512:(tt + 1) * 512],
                           t_ob[sl][:], s_y[sl], waits=[(s_dve, xf_rel[sl])])
                ob_rel[sl] = (s_y[sl], yf)
                wdb_rel[sl] = dc_cnt
            down_tot = {"pe": s_pe.n, "dve": s_dve.n,
                        "y0": s_y[0].n, "y1": s_y[1].n}
            P4b.close()
        S4.close()

        b.op("sync", lambda e: e.nop(),
             waits=[(s_y[0], s_y[0].n), (s_y[1], s_y[1].n)] +
                   ([(s_dbg, s_dbg.n)] if s_dbg.n else []))

        for s in b.sems:
            s.h = root.enter_context(nc.semaphore(s.name))
        blk = root.enter_context(nc.Block())
        _realize(b, blk)

    nc.finalize()
    return nc



def _stage_weights(inputs):
    Wq = np.asarray(inputs["L0_attn_Wq"], np.float32)
    Wk = np.asarray(inputs["L0_attn_Wk"], np.float32)
    Wv = np.asarray(inputs["L0_attn_Wv"], np.float32)
    Wo = np.asarray(inputs["L0_attn_Wo"], np.float32)
    wg = np.asarray(inputs["L0_ffn_w_gate"], np.float32)
    wu = np.asarray(inputs["L0_ffn_w_up"], np.float32)
    wd = np.asarray(inputs["L0_ffn_w_down"], np.float32)

    def qkv_stage(W):
        return np.ascontiguousarray(
            W.reshape(DC, 128, DC, 128).transpose(0, 3, 2, 1)).reshape(DC, 128, D)

    def gu_stage(W):
        return np.ascontiguousarray(
            W.reshape(IC, 128, DC, 128).transpose(0, 3, 2, 1)).reshape(IC, 128, D)

    d = {
        "wq": qkv_stage(Wq), "wk": qkv_stage(Wk), "wv": qkv_stage(Wv),
        "wo": qkv_stage(Wo), "wg": gu_stage(wg), "wu": gu_stage(wu),
        "wd": np.ascontiguousarray(
            wd.reshape(DC, 128, IC, 128).transpose(0, 3, 2, 1)).reshape(DC, 128, INTER),
        "bg": np.ascontiguousarray(
            np.asarray(inputs["L0_ffn_b_gate"], np.float32).reshape(IC, 128).T),
        "bu": np.ascontiguousarray(
            np.asarray(inputs["L0_ffn_b_up"], np.float32).reshape(IC, 128).T),
        "bd": np.ascontiguousarray(
            np.asarray(inputs["L0_ffn_b_down"], np.float32).reshape(DC, 128).T),
        "ident": np.eye(128, dtype=ml_dtypes.bfloat16),
    }
    rp = np.zeros((128, 128), np.float32)
    rp[(np.arange(128) + 64) % 128, np.arange(128)] = 1.0
    d["rperm"] = rp.astype(ml_dtypes.bfloat16)
    return d


def _stage_core(x, core):
    p0 = (core - 1) * TOK
    xtok = np.zeros((CTX, D), np.float32)
    lo = max(p0, 0)
    xtok[lo - p0: CTX] = x[lo: p0 + CTX]
    xfm = np.ascontiguousarray(x[core * TOK:(core + 1) * TOK].T)

    pos = np.arange(p0, p0 + CTX, dtype=np.float32)
    pos = np.maximum(pos, 0.0).astype(np.float32)
    inv = (1.0 / (10000.0 ** (np.arange(0, DH, 2, dtype=np.float32) / DH))).astype(np.float32)
    fr = (pos[:, None] * inv[None, :]).astype(np.float32)
    emb = np.concatenate([fr, fr], axis=1)
    cosb = np.ascontiguousarray(np.cos(emb).T.astype(np.float32))
    sin = np.sin(emb).T.astype(np.float32)
    sinr = np.ascontiguousarray(
        np.concatenate([-sin[:64], sin[64:]], axis=0))

    NEG = np.float32(-1e30)
    r = np.arange(128)[:, None]
    mp = np.arange(1152)[None, :]
    m = np.zeros((NQC, 128, 1152), np.float32)
    for qc in range(NQC):
        allowed = (mp > r) & (mp <= r + WIN)
        if core == 0:
            allowed = allowed & ((qc * 128 + mp) >= TOK)
        m[qc] = np.where(allowed, np.float32(0), NEG)
    masks = np.ascontiguousarray(
        m.transpose(1, 0, 2).reshape(128, NQC * 1152)).astype(ml_dtypes.bfloat16)

    return {"xtok": xtok, "xfm": xfm, "cosb": cosb, "sinr": sinr, "masks": masks}


_NC_CACHE = {}


def _get_program(dbg=()):
    key = tuple(sorted(dbg))
    if key not in _NC_CACHE:
        _NC_CACHE[key] = build_program(dbg)
    return _NC_CACHE[key]


def kernel(**inputs):
    x = np.asarray(inputs["L0_attn_x"], np.float32).reshape(S, D)
    shared = _stage_weights(inputs)
    in_maps = []
    for c in range(NCORES):
        m = dict(shared)
        m.update(_stage_core(x, c))
        in_maps.append(m)
    nc = _get_program()
    res = run_bass_kernel_spmd(nc, in_maps, core_ids=list(range(NCORES)))
    out = np.empty((S, D), np.float32)
    for c in range(NCORES):
        out[c * TOK:(c + 1) * TOK] = np.asarray(res.results[c]["y"]).T
    return out.reshape(1, S, D)


# version 09 (final)
# speedup vs baseline: 1.2398x; 1.2398x over previous
import contextlib
import numpy as np
import ml_dtypes
import concourse.bass as bass
import concourse.mybir as mybir
from concourse.bass_utils import run_bass_kernel_spmd

F32 = mybir.dt.float32
BF16 = mybir.dt.bfloat16
AF = mybir.ActivationFunctionType
ALU = mybir.AluOpType
AX = mybir.AxisListType

D = 4096
S = 8192
NCORES = 8
TOK = 1024
CTX = 2048
H = 32
DH = 128
WIN = 1024
INTER = 11008
DC = D // 128
IC = INTER // 128
NQC = TOK // 128
NTILE = CTX // 128
ISCALE = 0.08838834764831845

ENGINES = ("sync", "scalar", "vector", "tensor", "gpsimd")


class _S:
    __slots__ = ("name", "n", "h")

    def __init__(self, name):
        self.name = name
        self.n = 0
        self.h = None


class _B:
    def __init__(self):
        self.ops = {e: [] for e in ENGINES}
        self.sems = []

    def sem(self, name):
        s = _S(name)
        self.sems.append(s)
        return s

    def op(self, eng, fn, waits=(), inc=None, inc_n=1):
        w = tuple((s, int(v)) for (s, v) in waits if v > 0)
        tgt = None
        if inc is not None:
            inc.n += inc_n
            tgt = (inc, inc_n)
        self.ops[eng].append((w, fn, tgt))
        return inc.n if inc is not None else 0

    def dma(self, q, out_ap, in_ap, sem, waits=()):
        return self.op(
            q,
            (lambda e, o=out_ap, i=in_ap: e.dma_start(out=o, in_=i)),
            waits, inc=sem, inc_n=16,
        )


def _realize(b, blk):
    for ename in ENGINES:
        ops = b.ops[ename]
        if not ops:
            continue

        def mk(ops):
            def body(eng):
                for waits, fn, inc in ops:
                    for s, v in waits:
                        eng.wait_ge(s.h, v)
                    ins = fn(eng)
                    if inc is not None:
                        ins.then_inc(inc[0].h, inc[1])
            return body

        getattr(blk, ename)(mk(ops))


def build_program(dbg=()):
    nc = bass.Bass("TRN2", target_bir_lowering=False)

    d_xtok = nc.dram_tensor("xtok", [CTX, D], F32, kind="ExternalInput").ap()
    d_xfm = nc.dram_tensor("xfm", [D, TOK], F32, kind="ExternalInput").ap()
    d_wq = nc.dram_tensor("wq", [DC, 128, D], F32, kind="ExternalInput").ap()
    d_wk = nc.dram_tensor("wk", [DC, 128, D], F32, kind="ExternalInput").ap()
    d_wv = nc.dram_tensor("wv", [DC, 128, D], F32, kind="ExternalInput").ap()
    d_wo = nc.dram_tensor("wo", [DC, 128, D], F32, kind="ExternalInput").ap()
    d_wg = nc.dram_tensor("wg", [IC, 128, D], F32, kind="ExternalInput").ap()
    d_wu = nc.dram_tensor("wu", [IC, 128, D], F32, kind="ExternalInput").ap()
    d_wd = nc.dram_tensor("wd", [DC, 128, INTER], F32, kind="ExternalInput").ap()
    d_bg = nc.dram_tensor("bg", [128, IC], F32, kind="ExternalInput").ap()
    d_bu = nc.dram_tensor("bu", [128, IC], F32, kind="ExternalInput").ap()
    d_bd = nc.dram_tensor("bd", [128, DC], F32, kind="ExternalInput").ap()
    d_cos = nc.dram_tensor("cosb", [128, CTX], F32, kind="ExternalInput").ap()
    d_sinr = nc.dram_tensor("sinr", [128, CTX], F32, kind="ExternalInput").ap()
    d_ident = nc.dram_tensor("ident", [128, 128], BF16, kind="ExternalInput").ap()
    d_rperm = nc.dram_tensor("rperm", [128, 128], BF16, kind="ExternalInput").ap()
    d_masks = nc.dram_tensor("masks", [128, NQC * 1152], BF16, kind="ExternalInput").ap()
    d_y = nc.dram_tensor("y", [D, TOK], F32, kind="ExternalOutput").ap()

    sk = "ExternalOutput" if "scratch" in dbg else "Internal"
    d_ksc = nc.dram_tensor("ksc", [H, 128, CTX], BF16, kind=sk).ap()
    d_vsc = nc.dram_tensor("vsc", [H, 128, CTX], BF16, kind=sk).ap()
    d_qsc = nc.dram_tensor("qsc", [H, 128, TOK], BF16, kind=sk).ap()
    d_xnsc = nc.dram_tensor("xnsc", [DC, 128, TOK], BF16, kind=sk).ap()
    d_wosc = nc.dram_tensor("wosc", [DC, 128, TOK], F32, kind=sk).ap()
    d_wdsc = nc.dram_tensor("wdsc", [DC, 128, INTER], BF16, kind="Internal").ap()
    d_attn = None
    if "attn" in dbg:
        d_attn = nc.dram_tensor("attnd", [128, H * TOK], BF16, kind="ExternalOutput").ap()

    b = _B()
    s_pe = b.sem("pe")
    s_dve = b.sem("dve")
    s_act = b.sem("act")
    s_gp = b.sem("gp")
    s_misc = b.sem("misc")
    s_xt = [b.sem(f"xt{i}") for i in range(2)]
    s_xnsp = b.sem("xnsp")
    s_wsl = [b.sem(f"wsl{i}") for i in range(2)]
    s_spl = [b.sem(f"spl{i}") for i in range(2)]
    s_kb = [b.sem(f"kb{i}") for i in range(4)]
    s_vb = [b.sem(f"vb{i}") for i in range(4)]
    s_qb = [b.sem(f"qb{i}") for i in range(4)]
    s_wob = [b.sem(f"wob{i}") for i in range(2)]
    s_wow = [b.sem(f"wow{i}") for i in range(2)]
    s_wdp = [b.sem(f"wdp{i}") for i in range(2)]
    s_wdw = [b.sem(f"wdw{i}") for i in range(2)]
    s_xnh = b.sem("xnh")
    s_wg = [b.sem(f"wg{i}") for i in range(2)]
    s_wu = [b.sem(f"wu{i}") for i in range(2)]
    s_wdb = [b.sem(f"wdb{i}") for i in range(2)]
    s_xf = [b.sem(f"xf{i}") for i in range(2)]
    s_wo2 = [b.sem(f"wo2{i}") for i in range(2)]
    s_y = [b.sem(f"y{i}") for i in range(2)]
    s_dbg = b.sem("dbg")

    root = contextlib.ExitStack()
    with root:
        t_cos = root.enter_context(nc.sbuf_tensor("t_cos", [128, CTX], F32))
        t_sinr = root.enter_context(nc.sbuf_tensor("t_sinr", [128, CTX], F32))
        t_id = root.enter_context(nc.sbuf_tensor("t_id", [128, 128], BF16))
        t_rp = root.enter_context(nc.sbuf_tensor("t_rp", [128, 128], BF16))
        t_bg = root.enter_context(nc.sbuf_tensor("t_bg", [128, IC], F32))
        t_bu = root.enter_context(nc.sbuf_tensor("t_bu", [128, IC], F32))
        t_bd = root.enter_context(nc.sbuf_tensor("t_bd", [128, DC], F32))

        t_eps = root.enter_context(nc.sbuf_tensor("t_eps", [128, 1], F32))
        b.op("vector", lambda e: e.memset(t_eps[:], 1e-5), inc=s_dve)

        b.dma("sync", t_cos[:], d_cos, s_misc)
        b.dma("sync", t_sinr[:], d_sinr, s_misc)
        b.dma("sync", t_id[:], d_ident, s_misc)
        b.dma("sync", t_rp[:], d_rperm, s_misc)
        b.dma("sync", t_bg[:], d_bg, s_misc)
        b.dma("sync", t_bu[:], d_bu, s_misc)
        b.dma("sync", t_bd[:], d_bd, s_misc)
        CONST_TOT = s_misc.n

        S1 = contextlib.ExitStack()
        t_xn = S1.enter_context(nc.sbuf_tensor("t_xn", [128, DC * CTX], BF16))

        P1 = contextlib.ExitStack()
        t_xt = [P1.enter_context(nc.sbuf_tensor(f"t_xt{i}", [128, D], F32)) for i in range(2)]
        t_xnt = [P1.enter_context(nc.sbuf_tensor(f"t_xnt{i}", [128, D], BF16)) for i in range(2)]
        t_sum = [P1.enter_context(nc.sbuf_tensor(f"t_sum{i}", [128, 1], F32)) for i in range(2)]
        t_sq = [P1.enter_context(nc.sbuf_tensor(f"t_sq{i}", [128, 1], F32)) for i in range(2)]
        t_mean = [P1.enter_context(nc.sbuf_tensor(f"t_mean{i}", [128, 1], F32)) for i in range(2)]
        t_var = [P1.enter_context(nc.sbuf_tensor(f"t_var{i}", [128, 1], F32)) for i in range(2)]
        t_rstd = [P1.enter_context(nc.sbuf_tensor(f"t_rstd{i}", [128, 1], F32)) for i in range(2)]
        t_nb = [P1.enter_context(nc.sbuf_tensor(f"t_nb{i}", [128, 1], F32)) for i in range(2)]
        p_tr = [P1.enter_context(nc.psum_tensor(f"p_tr{i}", [128, 1024], BF16)) for i in range(2)]

        xtok_r = d_xtok.rearrange("(n p) d -> n p d", p=128)
        xn_r3 = t_xn[:].rearrange("p (dc t) -> p dc t", dc=DC)

        norm_cnt = [0, 0]
        red_cnt = [0, 0]
        trg_cnt = {}
        evac_cnt = {}
        sqw_cnt = [0, 0]
        m2w_cnt = [0, 0]
        lnr_cnt = [0, 0]
        nbw_cnt = [0, 0]
        pn_free = {}

        for i in range(NTILE):
            sl = i % 2
            w = []
            if i >= 2:
                w += [(s_act, norm_cnt[sl]), (s_dve, red_cnt[sl])]
            fills = b.dma("sync", t_xt[sl][:], xtok_r[i], s_xt[sl], waits=w)
            red_cnt[sl] = b.op(
                "vector",
                lambda e, o=t_sum[sl], x=t_xt[sl]: e.reduce_sum(o[:], x[:], axis=AX.X),
                waits=[(s_xt[sl], fills), (s_dve, m2w_cnt[sl])],
                inc=s_dve)
            w = [(s_xt[sl], fills), (s_dve, sqw_cnt[sl])]
            if i >= 2:
                w.append((s_pe, trg_cnt[(i - 2, 3)]))
            sq_cnt = b.op(
                "scalar",
                lambda e, o=t_xnt[sl], x=t_xt[sl], a=t_sq[sl]: e.activation(
                    o[:], x[:], AF.Square, accum_out=a[:]),
                waits=w, inc=s_act)
            sqw_cnt[sl] = b.op(
                "vector",
                lambda e, o=t_var[sl], x=t_sq[sl]: e.tensor_scalar_mul(o[:], x[:], 1.0 / D),
                waits=[(s_act, sq_cnt), (s_act, lnr_cnt[sl])], inc=s_dve)
            mean_cnt = b.op(
                "vector",
                lambda e, o=t_mean[sl], x=t_sum[sl]: e.tensor_scalar_mul(o[:], x[:], 1.0 / D),
                waits=[(s_dve, red_cnt[sl])], inc=s_dve)
            m2_cnt = b.op(
                "vector",
                lambda e, o=t_sum[sl], m=t_mean[sl]: e.tensor_mul(o[:], m[:], m[:]),
                waits=[(s_dve, mean_cnt)], inc=s_dve)
            m2w_cnt[sl] = m2_cnt
            var_cnt = b.op(
                "vector",
                lambda e, o=t_var[sl], m2=t_sum[sl]: e.tensor_sub(o[:], o[:], m2[:]),
                waits=[(s_dve, m2_cnt), (s_dve, sqw_cnt[sl])], inc=s_dve)
            w = [(s_dve, var_cnt), (s_dve, nbw_cnt[sl])]
            ln_cnt = b.op(
                "scalar",
                lambda e, o=t_rstd[sl], v=t_var[sl]: e.activation(o[:], v[:], AF.Ln, bias=t_eps[:]),
                waits=w, inc=s_act)
            lnr_cnt[sl] = b.op(
                "scalar",
                lambda e, o=t_rstd[sl]: e.activation(o[:], o[:], AF.Exp, scale=-0.5),
                waits=[(s_act, ln_cnt)], inc=s_act)
            w = [(s_act, lnr_cnt[sl])]
            if i >= 2:
                w.append((s_act, norm_cnt[sl]))
            w.append((s_dve, mean_cnt))
            nbw_cnt[sl] = b.op(
                "vector",
                lambda e, o=t_nb[sl], m=t_mean[sl], r=t_rstd[sl]: e.scalar_tensor_tensor(
                    o[:], m[:], -1.0, r[:], ALU.mult, ALU.mult),
                waits=w, inc=s_dve)
            norm_cnt[sl] = b.op(
                "scalar",
                lambda e, o=t_xnt[sl], x=t_xt[sl], r=t_rstd[sl], nb=t_nb[sl]: e.activation(
                    o[:], x[:], AF.Identity, bias=nb[:], scale=r[:]),
                waits=[(s_dve, nbw_cnt[sl]), (s_act, lnr_cnt[sl])], inc=s_act)
            for g in range(4):
                pg = p_tr[g % 2]
                w = [(s_act, norm_cnt[sl]), (s_misc, CONST_TOT)]
                key = (i, g - 2) if g >= 2 else (i - 1, g + 2)
                if key in evac_cnt:
                    w.append((s_dve, evac_cnt[key]))

                def tr_group(e, pg=pg, src=t_xnt[sl], g=g):
                    ins = None
                    for j in range(8):
                        ins = e.transpose(
                            pg[:, j * 128:(j + 1) * 128],
                            src[:, (g * 8 + j) * 128:(g * 8 + j + 1) * 128],
                            t_id[:])
                    return ins
                trg_cnt[(i, g)] = b.op("tensor", tr_group, waits=w, inc=s_pe)
                dst = xn_r3[:, g * 8:(g + 1) * 8, i * 128:(i + 1) * 128]
                src = pg[:].rearrange("p (a c) -> p a c", a=8)
                evac_cnt[(i, g)] = b.op(
                    "vector",
                    lambda e, d=dst, s=src: e.tensor_copy(d, s),
                    waits=[(s_pe, trg_cnt[(i, g)])], inc=s_dve)

        DVE_P1 = s_dve.n
        ACT_P1 = s_act.n
        PE_P1 = s_pe.n
        last_evac = s_dve.n
        for dc in range(DC):
            b.dma("sync", d_xnsc[dc],
                  t_xn[:, dc * CTX + TOK: dc * CTX + CTX],
                  s_xnsp, waits=[(s_dve, last_evac)] if dc == 0 else ())
        P1.close()

        P2a = contextlib.ExitStack()
        t_wsl = [P2a.enter_context(nc.sbuf_tensor(f"t_wsl{i}", [128, D], BF16)) for i in range(2)]
        t_ksb = [P2a.enter_context(nc.sbuf_tensor(f"t_ksb{i}", [128, 512], BF16)) for i in range(3)]
        t_t1 = [P2a.enter_context(nc.sbuf_tensor(f"t_t1{i}", [128, 512], F32)) for i in range(3)]
        t_t2 = [P2a.enter_context(nc.sbuf_tensor(f"t_t2{i}", [128, 512], F32)) for i in range(3)]
        t_ko = [P2a.enter_context(nc.sbuf_tensor(f"t_ko{i}", [128, CTX], BF16)) for i in range(2)]
        p_pk = [P2a.enter_context(nc.psum_tensor(f"p_pk{i}", [128, 512], F32)) for i in range(3)]
        p_rt = [P2a.enter_context(nc.psum_tensor(f"p_rt{i}", [128, 512], F32)) for i in range(3)]

        wsl_rel = [0, 0]
        wsl_fill = [0, 0]
        ko_rel = [(None, 0), (None, 0)]
        pk_rel = [(0, 0)] * 3
        rt_rel = [0, 0, 0]
        t12_rel = [0, 0, 0]
        ksb_rel = [(0, 0)] * 3
        uid = [0]

        def qkv_pass(kind):
            wd = {"k": d_wk, "v": d_wv, "q": d_wq}[kind]
            out_sc = {"k": d_ksc, "v": d_vsc, "q": d_qsc}[kind]
            ntt = 2 if kind == "q" else 4
            c0 = TOK if kind == "q" else 0
            rope = kind != "v"
            NU = H * ntt
            state = {}

            def rope_stage(up):
                h, tt, pb, slp, proj_cnt, ev = state[up]
                rot_cnt = b.op(
                    "tensor",
                    lambda e, o=p_rt[pb], i=t_ksb[pb]: e.matmul(
                        o[:], t_rp[:], i[:], start=True, stop=True),
                    waits=[(s_act, ev), (s_dve, rt_rel[pb])], inc=s_pe)
                cs = c0 + tt * 512
                cos_cnt = b.op(
                    "vector",
                    lambda e, o=t_t1[pb], p=t_ksb[pb], cs=cs: e.tensor_mul(
                        o[:], p[:], t_cos[:, cs:cs + 512]),
                    waits=[(s_act, ev), (s_dve, t12_rel[pb])], inc=s_dve)
                ksb_rel[pb] = (rot_cnt, cos_cnt)
                sin_cnt = b.op(
                    "vector",
                    lambda e, o=t_t2[pb], p=p_rt[pb], cs=cs: e.tensor_mul(
                        o[:], p[:], t_sinr[:, cs:cs + 512]),
                    waits=[(s_pe, rot_cnt)], inc=s_dve)
                rt_rel[pb] = sin_cnt
                w2 = [(s_dve, sin_cnt)]
                if tt == 0 and ko_rel[slp][0] is not None:
                    w2.append(ko_rel[slp])
                piece = b.op(
                    "vector",
                    lambda e, o=t_ko[slp], a=t_t1[pb], c=t_t2[pb], tt=tt: e.tensor_add(
                        o[:, tt * 512:(tt + 1) * 512], a[:], c[:]),
                    waits=w2, inc=s_dve)
                t12_rel[pb] = piece
                if tt == ntt - 1:
                    spl_f = b.dma("sync", out_sc[h], t_ko[slp][:, 0:ntt * 512],
                                  s_spl[slp], waits=[(s_dve, piece)])
                    ko_rel[slp] = (s_spl[slp], spl_f)

            for u in range(NU + 1):
                if u < NU:
                    h, tt = u // ntt, u % ntt
                    sl = h % 2
                    if tt == 0:
                        w = [(s_pe, wsl_rel[sl])]
                        if kind == "k" and h < 2:
                            w += [(s_act, ACT_P1), (s_pe, PE_P1), (s_dve, DVE_P1)]
                        wsl_fill[sl] = b.dma("gpsimd", t_wsl[sl][:], wd[h],
                                             s_wsl[sl], waits=w)
                    uu = uid[0]
                    uid[0] += 1
                    pb = uu % 3
                    pk = p_pk[pb]
                    w = [(s_wsl[sl], wsl_fill[sl]),
                         (s_act, pk_rel[pb][0]), (s_dve, pk_rel[pb][1])]
                    if kind == "k" and h == 0 and tt == 0:
                        w.append((s_dve, DVE_P1))

                    def proj(e, pk=pk, wb=t_wsl[sl], c0=c0, tt=tt):
                        ins = None
                        for dc in range(DC):
                            ins = e.matmul(
                                pk[:],
                                wb[:, dc * 128:(dc + 1) * 128],
                                t_xn[:, dc * CTX + c0 + tt * 512: dc * CTX + c0 + (tt + 1) * 512],
                                start=(dc == 0), stop=(dc == DC - 1))
                        return ins
                    proj_cnt = b.op("tensor", proj, waits=w, inc=s_pe)
                    if tt == ntt - 1:
                        wsl_rel[sl] = proj_cnt
                    if rope:
                        ev = b.op(
                            "scalar",
                            lambda e, o=t_ksb[pb], p=pk: e.activation(o[:], p[:], AF.Identity),
                            waits=[(s_pe, proj_cnt), (s_pe, ksb_rel[pb][0]),
                                   (s_dve, ksb_rel[pb][1])], inc=s_act)
                        pk_rel[pb] = (ev, 0)
                        state[u] = (h, tt, pb, sl, proj_cnt, ev)
                    else:
                        w2 = [(s_pe, proj_cnt)]
                        if tt == 0 and ko_rel[sl][0] is not None:
                            w2.append(ko_rel[sl])
                        piece = b.op(
                            "scalar",
                            lambda e, o=t_ko[sl], p=pk, tt=tt: e.activation(
                                o[:, tt * 512:(tt + 1) * 512], p[:], AF.Identity),
                            waits=w2, inc=s_act)
                        pk_rel[pb] = (piece, pk_rel[pb][1])
                        if tt == ntt - 1:
                            spl_f = b.dma("sync", out_sc[h], t_ko[sl][:, 0:ntt * 512],
                                          s_spl[sl], waits=[(s_act, piece)])
                            ko_rel[sl] = (s_spl[sl], spl_f)
                if rope and u >= 1 and (u - 1) in state:
                    rope_stage(u - 1)
                    del state[u - 1]

        qkv_pass("k")
        qkv_pass("v")
        qkv_pass("q")

        PE_2A = s_pe.n
        DVE_2A = s_dve.n
        ACT_2A = s_act.n
        SPL_TOT = (s_spl[0].n, s_spl[1].n)
        P2a.close()
        S1.close()

        S2 = contextlib.ExitStack()
        t_att = S2.enter_context(nc.sbuf_tensor("t_att", [128, H * TOK], BF16))

        P2b = contextlib.ExitStack()
        t_msk = P2b.enter_context(nc.sbuf_tensor("t_msk", [128, NQC * 1152], BF16))
        t_kb = [P2b.enter_context(nc.sbuf_tensor(f"t_kb{i}", [128, CTX], BF16)) for i in range(4)]
        t_vb = [P2b.enter_context(nc.sbuf_tensor(f"t_vb{i}", [128, CTX], BF16)) for i in range(4)]
        t_qb = [P2b.enter_context(nc.sbuf_tensor(f"t_qb{i}", [128, TOK], BF16)) for i in range(4)]
        t_vkm = [P2b.enter_context(nc.sbuf_tensor(f"t_vkm{i}", [128, CTX], BF16)) for i in range(2)]
        t_ssb = [P2b.enter_context(nc.sbuf_tensor(f"t_ssb{i}", [128, 1152], F32)) for i in range(2)]
        t_pn = [P2b.enter_context(nc.sbuf_tensor(f"t_pn{i}", [128, 1152], BF16)) for i in range(4)]
        t_pT = [P2b.enter_context(nc.sbuf_tensor(f"t_pT{i}", [128, 1152], BF16)) for i in range(2)]
        t_nm = [P2b.enter_context(nc.sbuf_tensor(f"t_nm{i}", [128, 1], F32)) for i in range(2)]
        t_su = [P2b.enter_context(nc.sbuf_tensor(f"t_su{i}", [128, 1], F32)) for i in range(2)]
        t_rc = [P2b.enter_context(nc.sbuf_tensor(f"t_rc{i}", [128, 1], F32)) for i in range(2)]
        t_wdp = [P2b.enter_context(nc.sbuf_tensor(f"t_wdp{i}", [128, 5504], BF16)) for i in range(2)]
        p_sc = P2b.enter_context(nc.psum_tensor("p_sc", [128, 1152], F32))
        p_pT = P2b.enter_context(nc.psum_tensor("p_pT", [128, 1152], BF16))
        p_av = P2b.enter_context(nc.psum_tensor("p_av", [128, 128], F32))
        p_vt = [P2b.enter_context(nc.psum_tensor(f"p_vt{i}", [128, 1024], BF16)) for i in range(2)]

        bar2b = [(s_pe, PE_2A), (s_dve, DVE_2A), (s_act, ACT_2A),
                 (s_spl[0], SPL_TOT[0]), (s_spl[1], SPL_TOT[1]),
                 (s_xnsp, 16 * DC)]
        b.dma("sync", t_msk[:], d_masks, s_misc, waits=bar2b)
        MSK_TOT = s_misc.n

        wdp_rel = [(None, 0), (None, 0)]
        for u2 in range(2 * DC):
            fc, hf = u2 // 2, u2 % 2
            sl = u2 % 2
            w = list(bar2b) if u2 < 2 else []
            if wdp_rel[sl][0] is not None:
                w.append(wdp_rel[sl])
            f = b.dma("gpsimd", t_wdp[sl][:],
                      d_wd[fc][:, hf * 5504:(hf + 1) * 5504], s_wdp[sl], waits=w)
            fw = b.dma("gpsimd", d_wdsc[fc][:, hf * 5504:(hf + 1) * 5504],
                       t_wdp[sl][:], s_wdw[sl], waits=[(s_wdp[sl], f)])
            wdp_rel[sl] = (s_wdw[sl], fw)
        WDW_TOT = (s_wdw[0].n, s_wdw[1].n)

        kb_rel = [0, 0, 0, 0]
        vb_rel = [0, 0, 0, 0]
        qb_rel = [0, 0, 0, 0]
        vkm_rel = [0, 0]
        vt_rel = [0, 0]
        sc_rel = 0
        ssb_rel = [0, 0]
        pT_rel = 0
        pTs_rel = [0, 0]
        pn_rel = [0, 0, 0, 0]
        av_rel = 0
        nm_rel = [0, 0]
        su_rel = [0, 0]
        rc_rel = [0, 0]
        av_done = {}
        pT_ev = {}

        for hp in range(H // 2):
            heads = (2 * hp, 2 * hp + 1)
            kbf = {}
            vbf = {}
            qbf = {}
            for h in heads:
                sl = h % 4
                w = list(bar2b) if hp == 0 else [(s_pe, kb_rel[sl])]
                kbf[h] = b.dma("sync", t_kb[sl][:], d_ksc[h], s_kb[sl], waits=w)
                w = list(bar2b) if hp == 0 else [(s_pe, vb_rel[sl])]
                vbf[h] = b.dma("sync", t_vb[sl][:], d_vsc[h], s_vb[sl], waits=w)
                w = list(bar2b) if hp == 0 else [(s_pe, qb_rel[sl])]
                qbf[h] = b.dma("sync", t_qb[sl][:], d_qsc[h], s_qb[sl], waits=w)

            vkm_cnt = {}
            for hi, h in enumerate(heads):
                sl = h % 4
                for g in range(2):
                    w = [(s_vb[sl], vbf[h]), (s_misc, CONST_TOT),
                         (s_act, vt_rel[g]), (s_pe, vkm_rel[hi])]

                    def vtr(e, g=g, vb=t_vb[sl]):
                        ins = None
                        for j2 in range(8):
                            kc = g * 8 + j2
                            ins = e.transpose(
                                p_vt[g][:, j2 * 128:(j2 + 1) * 128],
                                vb[:, kc * 128:(kc + 1) * 128], t_id[:])
                        return ins
                    c = b.op("tensor", vtr, waits=w, inc=s_pe)
                    vt_rel[g] = b.op(
                        "scalar",
                        lambda e, g=g, vk=t_vkm[hi]: e.activation(
                            vk[:, g * 1024:(g + 1) * 1024], p_vt[g][:], AF.Identity),
                        waits=[(s_pe, c)], inc=s_act)
                vkm_cnt[h] = vt_rel[1]
                vb_rel[sl] = c

            pn_cnt = {}
            exp_cnt = {}
            rc_cnt = {}
            NJ = 2 * NQC
            for j in range(NJ + 3):
                if 2 <= j <= NJ + 1:
                    jb = j - 2
                    q4 = jb % 4
                    q2 = jb % 2
                    w = [(s_act, exp_cnt[jb]), (s_act, pT_rel)]

                    def ptr(e, pn=t_pn[q4]):
                        ins = None
                        for j2 in range(9):
                            ins = e.transpose(
                                p_pT[:, j2 * 128:(j2 + 1) * 128],
                                pn[:, j2 * 128:(j2 + 1) * 128], t_id[:])
                        return ins
                    pn_rel[q4] = b.op("tensor", ptr, waits=w, inc=s_pe)
                    pT_rel = b.op(
                        "scalar",
                        lambda e, pT=t_pT[q2]: e.activation(pT[:], p_pT[:], AF.Identity),
                        waits=[(s_pe, pn_rel[q4]), (s_pe, pTs_rel[q2])], inc=s_act)
                    pTev_cnt = {}
                    pTev_cnt[jb] = pT_rel
                    pT_ev[jb] = pT_rel
                if j < NJ:
                    hi = j % 2
                    h = heads[hi]
                    qc = j // 2
                    sl = h % 4
                    q2 = j % 2
                    q4 = j % 4
                    w = [(s_qb[sl], qbf[h]), (s_kb[sl], kbf[h]), (s_dve, sc_rel)]

                    def scores(e, qb=t_qb[sl], kb=t_kb[sl], qc=qc):
                        base = qc * 128
                        ins = None
                        for off, wd in ((0, 512), (512, 512), (1024, 128)):
                            ins = e.matmul(
                                p_sc[:, off:off + wd],
                                qb[:, qc * 128:(qc + 1) * 128],
                                kb[:, base + off:base + off + wd],
                                start=True, stop=True)
                        return ins
                    sc_cnt = b.op("tensor", scores, waits=w, inc=s_pe)
                    w = [(s_pe, sc_cnt), (s_misc, MSK_TOT), (s_act, ssb_rel[q2])]

                    def mask3(e, qc=qc, ss=t_ssb[q2]):
                        ins = None
                        for off, wd in ((0, 512), (512, 512), (1024, 128)):
                            ins = e.scalar_tensor_tensor(
                                ss[:, off:off + wd],
                                p_sc[:, off:off + wd],
                                ISCALE,
                                t_msk[:, qc * 1152 + off: qc * 1152 + off + wd],
                                ALU.mult, ALU.add)
                        return ins
                    sc_rel = b.op("vector", mask3, waits=w, inc=s_dve)
                    negm_cnt = b.op(
                        "vector",
                        lambda e, o=t_nm[q2], ss=t_ssb[q2]: e.reduce_max(
                            o[:], ss[:], axis=AX.X, negate=True),
                        waits=[(s_act, nm_rel[q2]), (s_dve, sc_rel)], inc=s_dve)
                    w = [(s_dve, negm_cnt), (s_dve, su_rel[q2]),
                         (s_pe, pn_rel[q4])]
                    exp_cnt[j] = b.op(
                        "scalar",
                        lambda e, nb=t_nm[q2], su=t_su[q2], pn=t_pn[q4],
                        ss=t_ssb[q2]: e.activation(
                            pn[:], ss[:], AF.Exp, bias=nb[:], accum_out=su[:]),
                        waits=w, inc=s_act)
                    ssb_rel[q2] = exp_cnt[j]
                    nm_rel[q2] = exp_cnt[j]
                if 3 <= j:
                    jc = j - 3
                    hi = jc % 2
                    h = heads[hi]
                    qc = jc // 2
                    q2 = jc % 2
                    w = [(s_act, pT_ev[jc]), (s_act, av_rel)]
                    if qc == 0:
                        w.append((s_act, vkm_cnt[h]))

                    def av(e, qc=qc, pT=t_pT[q2], vk=t_vkm[hi]):
                        ins = None
                        for j2 in range(9):
                            kc = qc + j2
                            ins = e.matmul(
                                p_av[:],
                                pT[:, j2 * 128:(j2 + 1) * 128],
                                vk[:, kc * 128:(kc + 1) * 128],
                                start=(j2 == 0), stop=(j2 == 8))
                        return ins
                    pTs_rel[q2] = b.op("tensor", av, waits=w, inc=s_pe)
                    av_done[h] = pTs_rel[q2]
                    w = [(s_pe, pTs_rel[q2]), (s_dve, rc_cnt[jc])]
                    av_rel = b.op(
                        "scalar",
                        lambda e, r=t_rc[q2], h=h, qc=qc: e.activation(
                            t_att[:, qc * 4096 + h * 128: qc * 4096 + (h + 1) * 128],
                            p_av[:], AF.Identity, scale=r[:]),
                        waits=w, inc=s_act)
                    rc_rel[q2] = av_rel
                if 1 <= j <= NJ:
                    jr = j - 1
                    q2 = jr % 2
                    w = [(s_act, exp_cnt[jr]), (s_act, rc_rel[q2])]
                    su_rel[q2] = b.op(
                        "vector",
                        lambda e, o=t_rc[q2], i=t_su[q2]: e.reciprocal(o[:], i[:]),
                        waits=w, inc=s_dve)
                    rc_cnt[jr] = su_rel[q2]
            for hi, h in enumerate(heads):
                sl = h % 4
                kb_rel[sl] = av_done[h]
                qb_rel[sl] = av_done[h]
                vkm_rel[hi] = av_done[h]

        PE_2B = s_pe.n
        DVE_2B = s_dve.n
        ACT_2B = s_act.n
        if d_attn is not None:
            b.dma("sync", d_attn, t_att[:], s_dbg, waits=[(s_act, ACT_2B)])
        P2b.close()

        P3 = contextlib.ExitStack()
        t_wob = [P3.enter_context(nc.sbuf_tensor(f"t_wob{i}", [128, D], BF16)) for i in range(2)]
        t_wot = [P3.enter_context(nc.sbuf_tensor(f"t_wot{i}", [128, 512], F32)) for i in range(2)]
        p_wo = [P3.enter_context(nc.psum_tensor(f"p_wo{i}", [128, 512], F32)) for i in range(2)]

        bar3 = [(s_pe, PE_2B), (s_dve, DVE_2B), (s_act, ACT_2B)]
        t_afm = P3.enter_context(nc.sbuf_tensor("t_afm", [128, H * TOK], BF16))
        p_at = [P3.enter_context(nc.psum_tensor(f"p_at{i}", [128, 1024], BF16)) for i in range(2)]
        at_ev = [0, 0]
        for u3 in range(32):
            qc3, hg = u3 // 4, u3 % 4
            w = list(bar3) + [(s_misc, CONST_TOT), (s_dve, at_ev[u3 % 2])]

            def atr(e, qc3=qc3, hg=hg, pg=p_at[u3 % 2]):
                ins = None
                for j3 in range(8):
                    h3 = hg * 8 + j3
                    ins = e.transpose(
                        pg[:, j3 * 128:(j3 + 1) * 128],
                        t_att[:, qc3 * 4096 + h3 * 128: qc3 * 4096 + (h3 + 1) * 128],
                        t_id[:])
                return ins
            c3 = b.op("tensor", atr, waits=w, inc=s_pe)
            dst = t_afm[:].rearrange("p (h t) -> p h t", h=H)[
                :, hg * 8:(hg + 1) * 8, qc3 * 128:(qc3 + 1) * 128]
            at_ev[u3 % 2] = b.op(
                "vector",
                lambda e, d=dst, pg=p_at[u3 % 2]: e.tensor_copy(
                    d, pg[:].rearrange("p (a c) -> p a c", a=8)),
                waits=[(s_pe, c3)], inc=s_dve)
        ATT_FM_DONE = s_dve.n
        wob_rel = [0, 0]
        wot_rel = [(None, 0), (None, 0)]
        pwo_rel = [0, 0]
        for fc in range(DC):
            sl = fc % 2
            w = list(bar3) if fc < 2 else [(s_pe, wob_rel[sl])]
            wobf = b.dma("gpsimd", t_wob[sl][:], d_wo[fc], s_wob[sl], waits=w)
            for tt in range(2):
                u = fc * 2 + tt
                pb = u % 2
                w = [(s_wob[sl], wobf), (s_act, pwo_rel[pb])]
                if fc == 0 and tt == 0:
                    w.append((s_dve, ATT_FM_DONE))

                def womm(e, wb=t_wob[sl], tt=tt, pb=pb):
                    ins = None
                    for dc in range(DC):
                        ins = e.matmul(
                            p_wo[pb][:],
                            wb[:, dc * 128:(dc + 1) * 128],
                            t_afm[:, dc * TOK + tt * 512: dc * TOK + (tt + 1) * 512],
                            start=(dc == 0), stop=(dc == DC - 1))
                    return ins
                mmc = b.op("tensor", womm, waits=w, inc=s_pe)
                w = [(s_pe, mmc)]
                if wot_rel[pb][0] is not None:
                    w.append(wot_rel[pb])
                pwo_rel[pb] = b.op(
                    "scalar",
                    lambda e, o=t_wot[pb], p=p_wo[pb]: e.activation(o[:], p[:], AF.Identity),
                    waits=w, inc=s_act)
                wf = b.dma("sync", d_wosc[fc][:, tt * 512:(tt + 1) * 512],
                           t_wot[pb][:], s_wow[pb], waits=[(s_act, pwo_rel[pb])])
                wot_rel[pb] = (s_wow[pb], wf)
            wob_rel[sl] = mmc

        PE_P3 = s_pe.n
        ACT_P3 = s_act.n
        WOW_TOT = (s_wow[0].n, s_wow[1].n)
        P3.close()
        S2.close()

        S4 = contextlib.ExitStack()
        t_h = S4.enter_context(nc.sbuf_tensor("t_h", [128, IC * 512], BF16))

        bar4 = [(s_pe, PE_P3), (s_act, ACT_P3),
                (s_wow[0], WOW_TOT[0]), (s_wow[1], WOW_TOT[1]),
                (s_wdw[0], WDW_TOT[0]), (s_wdw[1], WDW_TOT[1])]
        if s_dbg.n:
            bar4.append((s_dbg, s_dbg.n))
        gate_tot_pe = 0
        down_tot = {"pe": 0, "dve": 0, "y0": 0, "y1": 0}
        wdb_rel = [0, 0]
        for tt in range(2):
            P4a = contextlib.ExitStack()
            t_xnh = P4a.enter_context(nc.sbuf_tensor(f"t_xnh{tt}", [128, DC * 512], BF16))
            t_wgb = [P4a.enter_context(nc.sbuf_tensor(f"t_wgb{tt}_{i}", [128, D], BF16)) for i in range(2)]
            t_wub = [P4a.enter_context(nc.sbuf_tensor(f"t_wub{tt}_{i}", [128, D], BF16)) for i in range(2)]
            t_gs = [P4a.enter_context(nc.sbuf_tensor(f"t_gs{tt}_{i}", [128, 512], BF16)) for i in range(2)]
            t_us = [P4a.enter_context(nc.sbuf_tensor(f"t_us{tt}_{i}", [128, 512], BF16)) for i in range(2)]
            p_g = [P4a.enter_context(nc.psum_tensor(f"p_g{tt}_{i}", [128, 512], F32)) for i in range(2)]
            p_u = [P4a.enter_context(nc.psum_tensor(f"p_u{tt}_{i}", [128, 512], F32)) for i in range(2)]

            if tt == 0:
                barA = bar4
            else:
                barA = [(s_pe, down_tot["pe"]), (s_dve, down_tot["dve"]),
                        (s_y[0], down_tot["y0"]), (s_y[1], down_tot["y1"])]
            for dc in range(DC):
                w = list(barA) + [(s_xnsp, 16 * DC)] if dc == 0 else []
                if tt == 1 and dc == 0:
                    w.append((s_pe, gate_tot_pe))
                b.dma("sync", t_xnh[:, dc * 512:(dc + 1) * 512],
                      d_xnsc[dc][:, tt * 512:(tt + 1) * 512], s_xnh, waits=w)
            XNH_TOT = s_xnh.n

            wg_fill = [0, 0]
            wu_fill = [0, 0]
            wg_rel = [0, 0]
            gs_rel = [0, 0]
            pg_rel = [0, 0]
            pu_rel = [0, 0]
            mul_cnt = {}

            def load_wgu(ic):
                sl = ic % 2
                if ic < 2:
                    w = list(barA)
                else:
                    w = [(s_pe, wg_rel[sl])]
                wg_fill[sl] = b.dma("gpsimd", t_wgb[sl][:], d_wg[ic], s_wg[sl], waits=list(w))
                wu_fill[sl] = b.dma("gpsimd", t_wub[sl][:], d_wu[ic], s_wu[sl], waits=list(w))

            load_wgu(0)
            load_wgu(1)
            for ic in range(IC):
                sl = ic % 2
                w = [(s_wg[sl], wg_fill[sl]), (s_xnh, XNH_TOT),
                     (s_act, pg_rel[sl])]

                def gmm(e, wb=t_wgb[sl], pb=p_g[sl], xh=t_xnh):
                    ins = None
                    for dc in range(DC):
                        ins = e.matmul(
                            pb[:], wb[:, dc * 128:(dc + 1) * 128],
                            xh[:, dc * 512:(dc + 1) * 512],
                            start=(dc == 0), stop=(dc == DC - 1))
                    return ins
                gc = b.op("tensor", gmm, waits=w, inc=s_pe)
                w = [(s_wu[sl], wu_fill[sl]), (s_act, pu_rel[sl])]

                def umm(e, wb=t_wub[sl], pb=p_u[sl], xh=t_xnh):
                    ins = None
                    for dc in range(DC):
                        ins = e.matmul(
                            pb[:], wb[:, dc * 128:(dc + 1) * 128],
                            xh[:, dc * 512:(dc + 1) * 512],
                            start=(dc == 0), stop=(dc == DC - 1))
                    return ins
                uc = b.op("tensor", umm, waits=w, inc=s_pe)
                wg_rel[sl] = uc
                w = [(s_pe, gc), (s_gp, gs_rel[sl])]
                pg_rel[sl] = b.op(
                    "scalar",
                    lambda e, o=t_gs[sl], p=p_g[sl], ic=ic: e.activation(
                        o[:], p[:], AF.Silu, bias=t_bg[:, ic:ic + 1]),
                    waits=w, inc=s_act)
                pu_rel[sl] = b.op(
                    "scalar",
                    lambda e, o=t_us[sl], p=p_u[sl], ic=ic: e.activation(
                        o[:], p[:], AF.Identity, bias=t_bu[:, ic:ic + 1]),
                    waits=[(s_pe, uc)], inc=s_act)
                if ic + 2 < IC:
                    load_wgu(ic + 2)
                w = [(s_act, pu_rel[sl])]
                if tt == 1 and ic == 0:
                    w.append((s_pe, down_tot["pe"]))
                mul_cnt[ic] = b.op(
                    "gpsimd",
                    lambda e, g=t_gs[sl], u=t_us[sl], ic=ic: e.tensor_tensor(
                        t_h[:, ic * 512:(ic + 1) * 512], g[:], u[:], ALU.mult),
                    waits=w, inc=s_gp)
                gs_rel[sl] = mul_cnt[ic]
            gate_tot_pe = s_pe.n
            GATE_ACT = s_act.n
            MUL_TOT = s_gp.n
            P4a.close()

            P4b = contextlib.ExitStack()
            t_wdb = [P4b.enter_context(nc.sbuf_tensor(f"t_wdb{tt}_{i}", [128, INTER], BF16)) for i in range(2)]
            t_xf = [P4b.enter_context(nc.sbuf_tensor(f"t_xf{tt}_{i}", [128, 512], F32)) for i in range(2)]
            t_w2 = [P4b.enter_context(nc.sbuf_tensor(f"t_w2{tt}_{i}", [128, 512], F32)) for i in range(2)]
            t_ob = [P4b.enter_context(nc.sbuf_tensor(f"t_ob{tt}_{i}", [128, 512], F32)) for i in range(2)]
            p_d = [P4b.enter_context(nc.psum_tensor(f"p_d{tt}_{i}", [128, 512], F32)) for i in range(2)]

            barB = [(s_pe, gate_tot_pe), (s_act, GATE_ACT), (s_gp, MUL_TOT)]
            xf_rel = [0, 0]
            ob_rel = [(None, 0), (None, 0)]
            pd_rel = [0, 0]
            for fc in range(DC):
                sl = fc % 2
                w = [(s_pe, wdb_rel[sl])]
                if fc < 2:
                    w = list(barB) + [(s_pe, wdb_rel[sl])]
                    if tt == 0:
                        w += [(s_wdw[0], WDW_TOT[0]), (s_wdw[1], WDW_TOT[1])]
                wdbf = b.dma("sync", t_wdb[sl][:], d_wdsc[fc], s_wdb[sl], waits=w)
                w = [(s_dve, xf_rel[sl])] + (list(barB) if fc < 2 else [])
                xff = b.dma("scalar", t_xf[sl][:],
                            d_xfm[fc * 128:(fc + 1) * 128, tt * 512:(tt + 1) * 512],
                            s_xf[sl], waits=w)
                w = [(s_dve, xf_rel[sl])] + (list(barB) if fc < 2 else [])
                if tt == 0 and fc < 2:
                    w += [(s_wow[0], WOW_TOT[0]), (s_wow[1], WOW_TOT[1])]
                w2f = b.dma("scalar", t_w2[sl][:],
                            d_wosc[fc][:, tt * 512:(tt + 1) * 512], s_wo2[sl], waits=w)
                w = [(s_wdb[sl], wdbf), (s_gp, MUL_TOT), (s_dve, pd_rel[sl])]

                def dmm(e, wb=t_wdb[sl], pb=p_d[sl]):
                    ins = None
                    for ic in range(IC):
                        ins = e.matmul(
                            pb[:], wb[:, ic * 128:(ic + 1) * 128],
                            t_h[:, ic * 512:(ic + 1) * 512],
                            start=(ic == 0), stop=(ic == IC - 1))
                    return ins
                dc_cnt = b.op("tensor", dmm, waits=w, inc=s_pe)
                w = [(s_pe, dc_cnt), (s_xf[sl], xff)]
                if ob_rel[sl][0] is not None:
                    w.append(ob_rel[sl])
                o1_cnt = b.op(
                    "vector",
                    lambda e, o=t_ob[sl], p=p_d[sl], x=t_xf[sl], fc=fc: e.scalar_tensor_tensor(
                        o[:], p[:], t_bd[:, fc:fc + 1], x[:], ALU.add, ALU.add),
                    waits=w, inc=s_dve)
                xf_rel[sl] = b.op(
                    "vector",
                    lambda e, o=t_ob[sl], w2=t_w2[sl]: e.tensor_add(o[:], o[:], w2[:]),
                    waits=[(s_wo2[sl], w2f), (s_dve, o1_cnt)], inc=s_dve)
                pd_rel[sl] = xf_rel[sl]
                yf = b.dma("scalar", d_y[fc * 128:(fc + 1) * 128, tt * 512:(tt + 1) * 512],
                           t_ob[sl][:], s_y[sl], waits=[(s_dve, xf_rel[sl])])
                ob_rel[sl] = (s_y[sl], yf)
                wdb_rel[sl] = dc_cnt
            down_tot = {"pe": s_pe.n, "dve": s_dve.n,
                        "y0": s_y[0].n, "y1": s_y[1].n}
            P4b.close()
        S4.close()

        b.op("sync", lambda e: e.nop(),
             waits=[(s_y[0], s_y[0].n), (s_y[1], s_y[1].n)] +
                   ([(s_dbg, s_dbg.n)] if s_dbg.n else []))

        for s in b.sems:
            s.h = root.enter_context(nc.semaphore(s.name))
        blk = root.enter_context(nc.Block())
        _realize(b, blk)

    nc.finalize()
    return nc



def _stage_weights(inputs):
    Wq = np.asarray(inputs["L0_attn_Wq"], np.float32)
    Wk = np.asarray(inputs["L0_attn_Wk"], np.float32)
    Wv = np.asarray(inputs["L0_attn_Wv"], np.float32)
    Wo = np.asarray(inputs["L0_attn_Wo"], np.float32)
    wg = np.asarray(inputs["L0_ffn_w_gate"], np.float32)
    wu = np.asarray(inputs["L0_ffn_w_up"], np.float32)
    wd = np.asarray(inputs["L0_ffn_w_down"], np.float32)

    def qkv_stage(W):
        return np.ascontiguousarray(
            W.reshape(DC, 128, DC, 128).transpose(0, 3, 2, 1)).reshape(DC, 128, D)

    def gu_stage(W):
        return np.ascontiguousarray(
            W.reshape(IC, 128, DC, 128).transpose(0, 3, 2, 1)).reshape(IC, 128, D)

    d = {
        "wq": qkv_stage(Wq), "wk": qkv_stage(Wk), "wv": qkv_stage(Wv),
        "wo": qkv_stage(Wo), "wg": gu_stage(wg), "wu": gu_stage(wu),
        "wd": np.ascontiguousarray(
            wd.reshape(DC, 128, IC, 128).transpose(0, 3, 2, 1)).reshape(DC, 128, INTER),
        "bg": np.ascontiguousarray(
            np.asarray(inputs["L0_ffn_b_gate"], np.float32).reshape(IC, 128).T),
        "bu": np.ascontiguousarray(
            np.asarray(inputs["L0_ffn_b_up"], np.float32).reshape(IC, 128).T),
        "bd": np.ascontiguousarray(
            np.asarray(inputs["L0_ffn_b_down"], np.float32).reshape(DC, 128).T),
        "ident": np.eye(128, dtype=ml_dtypes.bfloat16),
    }
    rp = np.zeros((128, 128), np.float32)
    rp[(np.arange(128) + 64) % 128, np.arange(128)] = 1.0
    d["rperm"] = rp.astype(ml_dtypes.bfloat16)
    return d


def _stage_core(x, core):
    p0 = (core - 1) * TOK
    xtok = np.zeros((CTX, D), np.float32)
    lo = max(p0, 0)
    xtok[lo - p0: CTX] = x[lo: p0 + CTX]
    xfm = np.ascontiguousarray(x[core * TOK:(core + 1) * TOK].T)

    pos = np.arange(p0, p0 + CTX, dtype=np.float32)
    pos = np.maximum(pos, 0.0).astype(np.float32)
    inv = (1.0 / (10000.0 ** (np.arange(0, DH, 2, dtype=np.float32) / DH))).astype(np.float32)
    fr = (pos[:, None] * inv[None, :]).astype(np.float32)
    emb = np.concatenate([fr, fr], axis=1)
    cosb = np.ascontiguousarray(np.cos(emb).T.astype(np.float32))
    sin = np.sin(emb).T.astype(np.float32)
    sinr = np.ascontiguousarray(
        np.concatenate([-sin[:64], sin[64:]], axis=0))

    NEG = np.float32(-1e30)
    r = np.arange(128)[:, None]
    mp = np.arange(1152)[None, :]
    m = np.zeros((NQC, 128, 1152), np.float32)
    for qc in range(NQC):
        allowed = (mp > r) & (mp <= r + WIN)
        if core == 0:
            allowed = allowed & ((qc * 128 + mp) >= TOK)
        m[qc] = np.where(allowed, np.float32(0), NEG)
    masks = np.ascontiguousarray(
        m.transpose(1, 0, 2).reshape(128, NQC * 1152)).astype(ml_dtypes.bfloat16)

    return {"xtok": xtok, "xfm": xfm, "cosb": cosb, "sinr": sinr, "masks": masks}


_NC_CACHE = {}


def _get_program(dbg=()):
    key = tuple(sorted(dbg))
    if key not in _NC_CACHE:
        _NC_CACHE[key] = build_program(dbg)
    return _NC_CACHE[key]


def kernel(**inputs):
    x = np.asarray(inputs["L0_attn_x"], np.float32).reshape(S, D)
    shared = _stage_weights(inputs)
    in_maps = []
    for c in range(NCORES):
        m = dict(shared)
        m.update(_stage_core(x, c))
        in_maps.append(m)
    nc = _get_program()
    res = run_bass_kernel_spmd(nc, in_maps, core_ids=list(range(NCORES)))
    out = np.empty((S, D), np.float32)
    for c in range(NCORES):
        out[c * TOK:(c + 1) * TOK] = np.asarray(res.results[c]["y"]).T
    return out.reshape(1, S, D)
